# version 01
# baseline (speedup 1.0000x reference)
import contextlib
import numpy as np
import ml_dtypes
import concourse.bass as bass
import concourse.mybir as mybir
from concourse.bass_utils import run_bass_kernel_spmd

F32 = mybir.dt.float32
BF16 = mybir.dt.bfloat16
U8 = mybir.dt.uint8
AF = mybir.ActivationFunctionType
ALU = mybir.AluOpType
AX = mybir.AxisListType

T, D, H, PD, DH = 2048, 1024, 16, 4, 64
FF = 2816
NT = T // 128
NKC = D // 128
DECAY = 0.99
MAGIC = 12582912.0
EPS = 1e-5

PAIRS4 = [(0, 1), (0, 2), (0, 3), (1, 2), (1, 3), (2, 3)]
JW_PERM = [5, 4, 3, 2, 1, 0]
JW_SIGN = [1.0, -1.0, 1.0, 1.0, -1.0, 1.0]
IJ_PAIRS = [(a, b) for a in range(6) for b in range(a, 6)]
IC_CHUNKS = [(0, 512), (512, 512), (1024, 512), (1536, 512), (2048, 512), (2560, 256)]


def host_consts():
    ident = np.eye(128, dtype=np.float32)
    col = np.arange(128)[None, :]
    row = np.arange(128)[:, None]
    tri = (col >= row).astype(np.float32)
    esel = np.zeros((112, 16), np.float32)
    for h in range(16):
        esel[h * 7:(h + 1) * 7, h] = 1.0
    e2 = np.zeros((2, 128), np.float32)
    e2[0, :64] = 1.0
    e2[1, 64:] = 1.0
    return ident, tri, esel, e2


class Arena:
    def __init__(self, nc):
        self.nc = nc
        self.cur = 0
        self.peak = 0
        self.items = []
        self.marks = []

    def alloc(self, name, shape, dtype):
        per = int(np.prod(shape[1:])) * mybir.dt.size(dtype)
        addr = (self.cur + 31) // 32 * 32
        self.cur = addr + per
        self.peak = max(self.peak, self.cur)
        h = self.nc.virtual_sbuf_tensor(name, shape, dtype)
        self.items.append((h, addr, shape, dtype))
        return h

    def mark(self):
        self.marks.append(self.cur)

    def release(self):
        self.cur = self.marks.pop()

    def finalize(self, backing):
        off = self.nc.lookup_mloc(backing).addr
        for h, addr, shape, dtype in self.items:
            self.nc._tensor(h.name, list(shape), dtype, type="SB")
            mloc = self.nc.lookup_mloc(h)
            mloc.allocated = not self.nc.target_bir_lowering
            mloc.addr = off + addr


class Ctr:
    def __init__(self, h):
        self.h = h
        self.n = 0


def build_program(dbg=(), upto=99, skip=(), detect_races=True):
    nc = bass.Bass("TRN2", target_bir_lowering=False, detect_race_conditions=detect_races)

    def din(name, shape, dtype=F32):
        return nc.dram_tensor(name, list(shape), dtype, kind="ExternalInput").ap()

    x_d = din("x", [T, D])
    qkvw_d = din("qkv_w", [D, 3 * D])
    qkvb_d = din("qkv_b", [1, 3 * D])
    w1w_d = din("w1w", [D, 64])
    w2w_d = din("w2w", [D, 64])
    r1w_d = din("r1w", [D, 64])
    r2w_d = din("r2w", [D, 64])
    memvw_d = din("memv_w", [D, D])
    memvb_d = din("memv_b", [1, D])
    memgw_d = din("memg_w", [D, 16])
    memgb_d = din("memg_b", [1, 16])
    mscale_d = din("mem_scale", [16, 1])
    outw_d = din("out_w", [D, D])
    outb_d = din("out_b", [1, D])
    wg_d = din("w_gate", [FF, D])
    wu_d = din("w_up", [FF, D])
    wd_d = din("w_down", [D, FF])
    ident_d = din("c_ident", [128, 128])
    tri_d = din("c_tri", [128, 128])
    esel_d = din("c_esel", [112, 16])
    e2_d = din("c_e2", [2, 128])

    out_d = nc.dram_tensor("out", [T, D], F32, kind="ExternalOutput").ap()

    x2_dram = nc.dram_tensor("x2_scr", [T, D], F32, kind="Internal").ap()
    x2q_dram = nc.dram_tensor("x2q_scr", [T, D], BF16, kind="Internal").ap()
    yn_dram = nc.dram_tensor("yn_scr", [D, T], BF16, kind="Internal").ap()
    hq_dram = nc.dram_tensor("hq_scr", [T, FF], BF16, kind="Internal").ap()
    wgq_dram = nc.dram_tensor("wgq_scr", [FF, D], BF16, kind="Internal").ap()
    wuq_dram = nc.dram_tensor("wuq_scr", [FF, D], BF16, kind="Internal").ap()
    wdq_dram = nc.dram_tensor("wdq_scr", [D, FF], BF16, kind="Internal").ap()

    ops = {k: [] for k in "tvsgy"}
    def Te(f): ops['t'].append(f)
    def V(f): ops['v'].append(f)
    def S(f): ops['s'].append(f)
    def G(f): ops['g'].append(f)
    def Y(f): ops['y'].append(f)
    ENG = {"t": Te, "v": V, "s": S, "g": G, "y": Y}

    ctx = contextlib.ExitStack()
    ps = [ctx.enter_context(nc.psum_tensor(f"ps{i}", [128, 512], F32)) for i in range(8)]
    sem_names = ["qs", "qg", "qc", "t", "v", "s", "gc"]
    sems = {n: ctx.enter_context(nc.semaphore("sem_" + n)) for n in sem_names}
    cQS, cQG, cQC, cT, cV, cS, cGC = (Ctr(sems[n]) for n in sem_names)

    def wait(engkey, c, n=None):
        n = c.n if n is None else n
        if n > 0:
            ENG[engkey](lambda e, h=c.h, n=n: e.wait_ge(h, n))

    ar = Arena(nc)
    dbg_names = []

    def phase_barrier(engs=("y", "g")):
        for ek in engs:
            wait(ek, cT)
            wait(ek, cV)
            wait(ek, cS)
            wait(ek, cQG)
            wait(ek, cQS)
            wait(ek, cGC)

    def snap(name, src_ap, shape):
        if name not in dbg:
            return
        d = nc.dram_tensor("dbg_" + name, list(shape), F32, kind="ExternalOutput").ap()
        dbg_names.append(name)
        for ek in ("g",):
            wait(ek, cT)
            wait(ek, cV)
            wait(ek, cS)
            wait(ek, cQS)
        G(lambda e, d=d, src=src_ap: e.dma_start(out=d, in_=src).then_inc(cQG.h, 16))
        cQG.n += 16
        for ek in ("t", "v", "s", "y"):
            wait(ek, cQG)

    def dump(name, src_ap, shape, wait_ctrs):
        if name not in dbg:
            return
        d = nc.dram_tensor("dbg_" + name, list(shape), F32, kind="ExternalOutput").ap()
        dbg_names.append(name)
        for c in wait_ctrs:
            wait("g", c)
        G(lambda e, d=d, src=src_ap: e.dma_start(out=d, in_=src).then_inc(cQG.h, 16))
        cQG.n += 16

    ident_sb = ar.alloc("ident_sb", [128, 128], F32)
    tri_sb = ar.alloc("tri_sb", [128, 128], F32)
    esel_sb = ar.alloc("esel_sb", [112, 16], F32)
    e2_sb = ar.alloc("e2_sb", [2, 128], F32)
    ones1 = ar.alloc("ones1", [1, 128], F32)
    ones512 = ar.alloc("ones512", [1, 512], F32)
    sixt = ar.alloc("sixt", [16, 1], F32)
    mscale_sb = ar.alloc("mscale_sb", [16, 1], F32)
    memvb_sb = ar.alloc("memvb_sb", [1, D], F32)
    outb_sb = ar.alloc("outb_sb", [1, D], F32)
    memgb_sb = ar.alloc("memgb_sb", [1, 16], F32)

    for dst, src in [(ident_sb, ident_d), (tri_sb, tri_d), (esel_sb, esel_d),
                     (e2_sb, e2_d), (mscale_sb, mscale_d),
                     (memvb_sb, memvb_d), (outb_sb, outb_d), (memgb_sb, memgb_d)]:
        Y(lambda e, dst=dst, src=src: e.dma_start(out=dst[:], in_=src).then_inc(cQS.h, 16))
        cQS.n += 16
    G(lambda e: e.memset(ones1[:], 1.0))
    G(lambda e: e.memset(ones512[:], 1.0))
    G(lambda e: e.memset(sixt[:], 1.0 / 16.0).then_inc(cGC.h, 1))
    cGC.n += 1

    def quant_chain(src2d, n, A, Bt, C, Dt, mx, inv, hs, out3d, inc_after=True):
        nb = n // 16
        s3 = src2d.rearrange("p (a b) -> p a b", b=16)
        def mxb(t_):
            return t_[:, 0:nb].unsqueeze(2).broadcast_to([128, nb, 16])
        V(lambda e: e.tensor_reduce(mx[:, 0:nb], s3, AX.X, ALU.max,
                                    apply_absolute_value=True).then_inc(cV.h, 1))
        cV.n += 1
        wait("v", cV)
        V(lambda e: e.tensor_scalar(hs[:, 0:nb], mx[:, 0:nb], 1.0 / 6.0, 1e-12,
                                    ALU.mult, ALU.max).then_inc(cV.h, 1))
        cV.n += 1
        wait("v", cV)
        V(lambda e: e.reciprocal(inv[:, 0:nb], hs[:, 0:nb]).then_inc(cV.h, 1))
        cV.n += 1
        wait("v", cV)
        B3 = Bt[:, 0:n].rearrange("p (a b) -> p a b", b=16)
        V(lambda e: e.tensor_tensor(B3, s3, mxb(inv), ALU.mult))
        V(lambda e: e.tensor_scalar(A[:, 0:n], Bt[:, 0:n], 2.0, -2.0, ALU.min, ALU.max))
        V(lambda e: e.tensor_scalar(C[:, 0:n], Bt[:, 0:n], 0.5, 2.0, ALU.mult, ALU.min))
        V(lambda e: e.scalar_tensor_tensor(Dt[:, 0:n], C[:, 0:n], -2.0, A[:, 0:n], ALU.max, ALU.add))
        V(lambda e: e.scalar_tensor_tensor(C[:, 0:n], Bt[:, 0:n], 0.5, Dt[:, 0:n], ALU.mult, ALU.add))
        V(lambda e: e.tensor_scalar(Bt[:, 0:n], C[:, 0:n], MAGIC, MAGIC, ALU.add, ALU.subtract))
        V(lambda e: e.tensor_scalar(A[:, 0:n], Bt[:, 0:n], 4.0, -4.0, ALU.min, ALU.max))
        V(lambda e: e.tensor_scalar(C[:, 0:n], Bt[:, 0:n], 6.0, -6.0, ALU.min, ALU.max))
        V(lambda e: e.scalar_tensor_tensor(Dt[:, 0:n], Bt[:, 0:n], 2.0, C[:, 0:n], ALU.mult, ALU.subtract))
        V(lambda e: e.scalar_tensor_tensor(C[:, 0:n], A[:, 0:n], -0.5, Dt[:, 0:n], ALU.mult, ALU.add))
        C3 = C[:, 0:n].rearrange("p (a b) -> p a b", b=16)
        if inc_after:
            V(lambda e: e.tensor_tensor(out3d, C3, mxb(hs), ALU.mult).then_inc(cV.h, 1))
            cV.n += 1
        else:
            V(lambda e: e.tensor_tensor(out3d, C3, mxb(hs), ALU.mult))

    def ln_tile(r_ap, out_ap, bst, mv2, veps, sdv, rstd, mup):
        V(lambda e: e.bn_stats(bst[:, 0:6], r_ap[:, 0:512]))
        V(lambda e: e.bn_stats(bst[:, 6:12], r_ap[:, 512:1024]).then_inc(cV.h, 1))
        cV.n += 1
        wait("v", cV)
        V(lambda e: e.bn_aggr(mv2[:], bst[:]).then_inc(cV.h, 1))
        cV.n += 1
        wait("v", cV)
        V(lambda e: e.tensor_scalar(veps[:], mv2[:, 1:2], EPS, None, ALU.add).then_inc(cV.h, 1))
        cV.n += 1
        wait("s", cV)
        S(lambda e: e.activation(out=sdv[:], in_=veps[:], func=AF.Sqrt).then_inc(cS.h, 1))
        cS.n += 1
        wait("v", cS)
        V(lambda e: e.reciprocal(rstd[:], sdv[:]).then_inc(cV.h, 1))
        cV.n += 1
        wait("v", cV)
        V(lambda e: e.tensor_scalar(mup[:], mv2[:, 0:1], rstd[:, 0:1], -1.0, ALU.mult, ALU.mult).then_inc(cV.h, 1))
        cV.n += 1
        wait("s", cV)
        S(lambda e: e.activation(out=out_ap, in_=r_ap, func=AF.Identity,
                                 bias=mup[:], scale=rstd[:]).then_inc(cS.h, 1))
        cS.n += 1

    if upto >= 1:
        ar.mark()
        wfb = ar.alloc("wfb", [128, 2, FF], F32)
        wqo = ar.alloc("wqo", [128, 2, FF], BF16)
        qA = ar.alloc("qA", [128, 1408], F32)
        qB = ar.alloc("qB", [128, 1408], F32)
        qC = ar.alloc("qC", [128, 1408], F32)
        qD = ar.alloc("qD", [128, 1408], F32)
        qmx = ar.alloc("qmx", [128, 88], F32)
        qinv = ar.alloc("qinv", [128, 88], F32)
        qhs = ar.alloc("qhs", [128, 88], F32)

        qdone, sdone = {}, {}
        gidx = 0
        for w_src, w_dst, nrow, ncol in [(wg_d, wgq_dram, FF, D), (wu_d, wuq_dram, FF, D),
                                         (wd_d, wdq_dram, D, FF)]:
            ntile = nrow // 128
            nhalf = ncol if ncol <= 1408 else ncol // 2
            nchunk = ncol // nhalf
            for it in range(ntile):
                par = gidx % 2
                if gidx >= 2:
                    wait("y", cV, qdone[gidx - 2])
                Y(lambda e, w_src=w_src, it=it, par=par, ncol=ncol:
                  e.dma_start(out=wfb[:, par, 0:ncol],
                              in_=w_src[it * 128:(it + 1) * 128, :]).then_inc(cQS.h, 16))
                cQS.n += 16
                wait("v", cQS)
                if gidx >= 2:
                    wait("v", cQG, sdone[gidx - 2])
                for hf in range(nchunk):
                    quant_chain(wfb[:, par, hf * nhalf:(hf + 1) * nhalf], nhalf,
                                qA, qB, qC, qD, qmx, qinv, qhs,
                                wqo[:, par, hf * nhalf:(hf + 1) * nhalf].rearrange("p (a b) -> p a b", b=16),
                                inc_after=(hf == nchunk - 1))
                qdone[gidx] = cV.n
                wait("g", cV)
                G(lambda e, w_dst=w_dst, it=it, par=par, ncol=ncol:
                  e.dma_start(out=w_dst[it * 128:(it + 1) * 128, :],
                              in_=wqo[:, par, 0:ncol]).then_inc(cQG.h, 16))
                cQG.n += 16
                sdone[gidx] = cQG.n
                gidx += 1
        ar.release()

    xT = ar.alloc("xT", [128, NKC, T + 1], BF16)
    grow = ar.alloc("grow", [1, T], F32)
    wait("g", cQG)
    wait("g", cV)
    G(lambda e: e.memset(xT[:, :, 0:1], 0.0).then_inc(cGC.h, 1))
    cGC.n += 1
    qg_pads = cGC.n

    if upto >= 2:
        ar.mark()
        xtile = ar.alloc("xtile", [128, 2, D], F32)
        phase_barrier()
        trdone, evdone = {}, {}
        for it in range(NT):
            par = it % 2
            if it >= 2:
                wait("y", cT, trdone[it - 2])
            Y(lambda e, it=it, par=par:
              e.dma_start(out=xtile[:, par, :], in_=x_d[it * 128:(it + 1) * 128, :]).then_inc(cQS.h, 16))
            cQS.n += 16
            wait("t", cQS)
            wait("t", cGC, qg_pads)
            for gq in range(2):
                gi = it * 2 + gq
                bank = gi % 2
                if gi >= 2:
                    wait("t", cS, evdone[gi - 2])
                for q in range(4):
                    kc = gq * 4 + q
                    def tf(e, kc=kc, par=par, bank=bank, inc=(q == 3), q=q):
                        ins = e.transpose(ps[bank][:, q * 128:(q + 1) * 128],
                                          xtile[:, par, kc * 128:(kc + 1) * 128], ident_sb[:])
                        if inc:
                            ins.then_inc(cT.h, 1)
                    Te(tf)
                cT.n += 1
                wait("s", cT)
                S(lambda e, it=it, gq=gq, bank=bank:
                  e.activation(out=xT[:, gq * 4:(gq + 1) * 4, 1 + it * 128: 1 + (it + 1) * 128],
                               in_=ps[bank][:].rearrange("p (a b) -> p a b", b=128),
                               func=AF.Copy).then_inc(cS.h, 1))
                cS.n += 1
                evdone[gi] = cS.n
            trdone[it] = cT.n
        ar.release()
        dump("xT", xT[:], [128, NKC, T + 1], [cS])

    if upto >= 3:
        ar.mark()
        stm = ar.alloc("stm", [128, NT, 4, 64], F32)
        gpt = ar.alloc("gpt", [16, T], F32)

        ar.mark()
        smallw = ar.alloc("smallw", [128, NKC, 4, 64], BF16)
        memgw_sb = ar.alloc("memgw_sb", [128, NKC, 16], BF16)
        phase_barrier()
        for wi, wd_ in enumerate([w1w_d, w2w_d, r1w_d, r2w_d]):
            G(lambda e, wi=wi, wd_=wd_:
              e.dma_start(out=smallw[:, :, wi, :],
                          in_=wd_[:].rearrange("(k p) f -> p k f", p=128)).then_inc(cQG.h, 16))
            cQG.n += 16
        G(lambda e: e.dma_start(out=memgw_sb[:],
                                in_=memgw_d[:].rearrange("(k p) f -> p k f", p=128)).then_inc(cQG.h, 16))
        cQG.n += 16
        ldq = cQG.n
        evq = {}
        for it in (range(1) if "tm_one" in skip else (range(NT) if "tm_smalls" not in skip else [])):
            bank = it % 2
            wait("t", cQG, ldq)
            wait("t", cQS)
            wait("t", cS)
            if it >= 2:
                wait("t", cS, evq[it - 2])
            for w in range(4):
                shift = 0 if w == 0 else 1
                for kc in range(NKC):
                    last = (w == 3 and kc == NKC - 1)
                    def mf(e, w=w, kc=kc, it=it, bank=bank, shift=shift, last=last):
                        ins = e.matmul(ps[bank][:, w * 64:(w + 1) * 64],
                                       xT[:, kc, shift + it * 128: shift + it * 128 + 128],
                                       smallw[:, kc, w, :],
                                       start=(kc == 0), stop=(kc == NKC - 1),
                                       skip_group_check=True)
                        if last:
                            ins.then_inc(cT.h, 1)
                    Te(mf)
            cT.n += 1
            wait("s", cT)
            S(lambda e, it=it, bank=bank:
              e.activation(out=stm[:, it, :, :],
                           in_=ps[bank][:, 0:256].rearrange("p (a b) -> p a b", b=64),
                           func=AF.Copy).then_inc(cS.h, 1))
            cS.n += 1
            evq[it] = cS.n
        gpq = {}
        for tc in (range(4) if "gpt" not in skip else []):
            bank = 2 + tc % 2
            if tc >= 2:
                wait("t", cS, gpq[tc - 2])
            Te(lambda e, bank=bank:
               e.matmul(ps[bank][0:16, :], memgb_sb[:], ones512[:],
                        start=True, stop=False, skip_group_check=True))
            for kc in range(NKC):
                def gf(e, tc=tc, kc=kc, bank=bank, last=(kc == NKC - 1)):
                    ins = e.matmul(ps[bank][0:16, :], memgw_sb[:, kc, :],
                                   xT[:, kc, 1 + tc * 512: 1 + (tc + 1) * 512],
                                   start=False, stop=(kc == NKC - 1), skip_group_check=True)
                    if last:
                        ins.then_inc(cT.h, 1)
                Te(gf)
            cT.n += 1
            wait("s", cT)
            S(lambda e, tc=tc, bank=bank:
              e.activation(out=gpt[:, tc * 512:(tc + 1) * 512], in_=ps[bank][0:16, :],
                           func=AF.Copy).then_inc(cS.h, 1))
            cS.n += 1
            gpq[tc] = cS.n
        ar.release()
        dump("stm", stm[:], [128, NT, 4, 64], [cS])
        dump("gpt", gpt[:], [16, T], [cS])

        if upto >= 4:
            ar.mark()
            jwt = ar.alloc("jwt", [128, NT, 16, 6], F32)
            rlt = ar.alloc("rlt", [128, NT, 16, 6], F32)
            rl2t = ar.alloc("rl2t", [128, NT, 16, 6], F32)

            ar.mark()
            parts = ar.alloc("parts", [128, NT, 16, 6], F32)
            tmpA = ar.alloc("tmpA", [128, NT, 16], F32)
            tmpB = ar.alloc("tmpB", [128, NT, 16], F32)
            nsqA = ar.alloc("nsqA", [128, NT, 16], F32)
            nsqB = ar.alloc("nsqB", [128, NT, 16], F32)
            nrm = ar.alloc("nrm", [128, NT, 16], F32)
            rinv = ar.alloc("rinv", [128, NT, 16], F32)
            rinvn = ar.alloc("rinvn", [128, NT, 16], F32)

            wait("v", cS)
            wait("v", cT)

            def lines(side):
                a_i, b_i = (0, 1) if side == 0 else (2, 3)
                dst = jwt if side == 0 else rlt
                for p, (a, b) in enumerate(PAIRS4):
                    V(lambda e, a=a, b=b, a_i=a_i, b_i=b_i:
                      e.tensor_tensor(tmpA[:], stm[:, :, a_i, a::4], stm[:, :, b_i, b::4], ALU.mult))
                    V(lambda e, a=a, b=b, a_i=a_i, b_i=b_i:
                      e.tensor_tensor(tmpB[:], stm[:, :, a_i, b::4], stm[:, :, b_i, a::4], ALU.mult))
                    V(lambda e, p=p: e.tensor_tensor(parts[:, :, :, p], tmpA[:], tmpB[:], ALU.subtract))
                V(lambda e: e.tensor_tensor(nsqA[:], parts[:, :, :, 0], parts[:, :, :, 0], ALU.mult))
                for p in range(1, 6):
                    V(lambda e, p=p: e.tensor_tensor(tmpA[:], parts[:, :, :, p], parts[:, :, :, p], ALU.mult))
                    src, dst_n = (nsqA, nsqB) if p % 2 == 1 else (nsqB, nsqA)
                    if p < 5:
                        V(lambda e, src=src, dst_n=dst_n: e.tensor_tensor(dst_n[:], src[:], tmpA[:], ALU.add))
                    else:
                        V(lambda e, src=src, dst_n=dst_n:
                          e.tensor_tensor(dst_n[:], src[:], tmpA[:], ALU.add).then_inc(cV.h, 1))
                cV.n += 1
                wait("s", cV)
                S(lambda e: e.activation(out=nrm[:], in_=nsqB[:], func=AF.Sqrt).then_inc(cS.h, 1))
                cS.n += 1
                wait("v", cS)
                V(lambda e: e.tensor_scalar(nrm[:], nrm[:], 1e-12, None, ALU.max).then_inc(cV.h, 1))
                cV.n += 1
                wait("v", cV)
                V(lambda e: e.reciprocal(rinv[:], nrm[:]).then_inc(cV.h, 1))
                cV.n += 1
                wait("v", cV)
                if side == 0:
                    V(lambda e: e.tensor_scalar(rinvn[:], rinv[:], -1.0, None, ALU.mult))
                    for q in range(6):
                        p, sg = JW_PERM[q], JW_SIGN[q]
                        r = rinv if sg > 0 else rinvn
                        V(lambda e, q=q, p=p, r=r:
                          e.tensor_tensor(dst[:, :, :, q], parts[:, :, :, p], r[:], ALU.mult))
                else:
                    for q in range(6):
                        V(lambda e, q=q: e.tensor_tensor(dst[:, :, :, q], parts[:, :, :, q], rinv[:], ALU.mult))

            lines(0)
            lines(1)
            V(lambda e: e.tensor_scalar(rl2t[:], rlt[:], 2.0, None, ALU.mult))
            ar.release()

            ar.mark()
            ztm = ar.alloc("ztm", [128, 4, 16, 7], F32)
            utm = ar.alloc("utm", [128, 4, 16, 7], F32)
            Zfm = ar.alloc("Zfm", [112, T + 1], F32)
            Uc = ar.alloc("Uc", [112, 512], F32)
            Dc = ar.alloc("Dc", [112, 2, 512], F32)
            dec5 = ar.alloc("dec5", [112, 512], F32)
            UDc = ar.alloc("UDc", [112, 512], F32)

            phase_barrier(("g",))
            wait("v", cT)
            G(lambda e: e.memset(dec5[:], DECAY))
            G(lambda e: e.memset(Zfm[:, 0:1], 0.0).then_inc(cGC.h, 1))
            cGC.n += 1
            wait("v", cGC)

            trd = {}
            tix = 0
            for g in range(3):
                grp = IJ_PAIRS[g * 7:(g + 1) * 7]
                for tc in range(4):
                    for idx, (a, b) in enumerate(grp):
                        V(lambda e, a=a, b=b, idx=idx, tc=tc:
                          e.tensor_tensor(ztm[:, :, :, idx], jwt[:, tc * 4:(tc + 1) * 4, :, a],
                                          jwt[:, tc * 4:(tc + 1) * 4, :, b], ALU.mult))
                        ub = rlt if a == b else rl2t
                        def pu(e, a=a, b=b, idx=idx, tc=tc, ub=ub, last=(idx == len(grp) - 1)):
                            ins = e.tensor_tensor(utm[:, :, :, idx], rlt[:, tc * 4:(tc + 1) * 4, :, a],
                                                  ub[:, tc * 4:(tc + 1) * 4, :, b], ALU.mult)
                            if last:
                                ins.then_inc(cV.h, 1)
                        V(pu)
                    cV.n += 1
                    wait("t", cV)
                    for q in range(4):
                        for which in range(2):
                            src = ztm if which == 0 else utm
                            bank = tix % 2
                            if tix >= 2:
                                wait("t", cV, trd[tix - 2])
                            Te(lambda e, src=src, q=q, bank=bank:
                               e.transpose(ps[bank][0:112, 0:128],
                                           src[:, q, :, :].rearrange("p a b -> p (a b)"),
                                           ident_sb[:]).then_inc(cT.h, 1))
                            cT.n += 1
                            wait("v", cT)
                            if which == 0:
                                V(lambda e, q=q, tc=tc, bank=bank:
                                  e.tensor_copy(Zfm[:, 1 + tc * 512 + q * 128: 1 + tc * 512 + (q + 1) * 128],
                                                ps[bank][0:112, 0:128]).then_inc(cV.h, 1))
                            else:
                                V(lambda e, q=q, bank=bank:
                                  e.tensor_copy(Uc[:, q * 128:(q + 1) * 128],
                                                ps[bank][0:112, 0:128]).then_inc(cV.h, 1))
                            cV.n += 1
                            trd[tix] = cV.n
                            tix += 1
                    par = tc % 2
                    if tc == 0:
                        V(lambda e, par=par:
                          e.tensor_tensor_scan(Dc[:, par, :], dec5[:], Zfm[:, 0:512],
                                               0.0, ALU.mult, ALU.add))
                    else:
                        V(lambda e, tc=tc, par=par:
                          e.tensor_tensor_scan(Dc[:, par, :], dec5[:], Zfm[:, tc * 512:(tc + 1) * 512],
                                               Dc[:, 1 - par, 511:512], ALU.mult, ALU.add))
                    V(lambda e, par=par:
                      e.tensor_tensor(UDc[:], Uc[:], Dc[:, par, :], ALU.mult).then_inc(cV.h, 1))
                    cV.n += 1
                    wait("t", cV)
                    Te(lambda e, g=g, tc=tc:
                       e.matmul(ps[4 + tc][0:16, :], esel_sb[:], UDc[:],
                                start=(g == 0), stop=(g == 2),
                                skip_group_check=True).then_inc(cT.h, 1))
                    cT.n += 1
                    wait("v", cT)
            ar.release()

            ar.mark()
            sigms = ar.alloc("sigms", [16, T], F32)
            gm = ar.alloc("gm", [16, T], F32)
            gm2 = ar.alloc("gm2", [16, T], F32)
            wait("s", cT)
            wait("s", cV)
            for ci in range(4):
                S(lambda e, ci=ci:
                  e.activation(out=sigms[:, ci * 512:(ci + 1) * 512], in_=ps[4 + ci][0:16, :],
                               func=AF.Sigmoid, scale=mscale_sb[:]))
            S(lambda e: e.activation(out=gm[:], in_=gpt[:], func=AF.Sigmoid).then_inc(cS.h, 1))
            cS.n += 1
            wait("v", cS)
            V(lambda e: e.tensor_tensor(gm2[:], sigms[:], gm[:], ALU.mult).then_inc(cV.h, 1))
            cV.n += 1
            wait("t", cV)
            for ci in range(4):
                def mm2(e, ci=ci, last=(ci == 3)):
                    ins = e.matmul(ps[ci][0:1, :], sixt[:], gm2[:, ci * 512:(ci + 1) * 512],
                                   start=True, stop=True, skip_group_check=True)
                    if last:
                        ins.then_inc(cT.h, 1)
                Te(mm2)
            cT.n += 1
            wait("s", cT)
            for ci in range(4):
                def ge_(e, ci=ci, last=(ci == 3)):
                    ins = e.activation(out=grow[0:1, ci * 512:(ci + 1) * 512], in_=ps[ci][0:1, :],
                                       func=AF.Copy)
                    if last:
                        ins.then_inc(cS.h, 1)
                S(ge_)
            cS.n += 1
            dump("msc", sigms[:], [16, T], [cS])
            ar.release()
            ar.release()
            dump("grow", grow[:], [1, T], [cS])
        ar.release()

    if upto >= 5:
        ar.mark()
        KT = ar.alloc("KT", [128, NKC, T], BF16)
        vhat = ar.alloc("vhat", [128, NT, H, 65], BF16)
        qkvb_sb = ar.alloc("qkvb_sb", [1, 3 * D], F32)
        wq_res = ar.alloc("wq_res", [128, NKC, D], BF16)
        phase_barrier()
        Y(lambda e: e.dma_start(out=qkvb_sb[:], in_=qkvb_d).then_inc(cQS.h, 16))
        cQS.n += 16
        G(lambda e: e.memset(vhat[:, :, :, 64:65], 1.0).then_inc(cGC.h, 1))
        cGC.n += 1
        qg_vpad = cGC.n
        for kc in range(NKC):
            G(lambda e, kc=kc:
              e.dma_start(out=wq_res[:, kc, :],
                          in_=qkvw_d[kc * 128:(kc + 1) * 128, 0:1024]).then_inc(cQG.h, 16))
            cQG.n += 16

        ar.mark()
        wbuf = ar.alloc("wbuf", [128, NKC, 1024], BF16)
        phase_barrier(("g",))
        for kc in range(NKC):
            G(lambda e, kc=kc:
              e.dma_start(out=wbuf[:, kc, :],
                          in_=qkvw_d[kc * 128:(kc + 1) * 128, 1024:2048]).then_inc(cQG.h, 16))
            cQG.n += 16
        ldk = cQG.n
        for tc in range(4):
            for fc in range(NKC):
                bank = (tc * NKC + fc) % 2
                wait("t", cQG, ldk)
                wait("t", cQS)
                wait("t", cS)
                Te(lambda e, fc=fc, bank=bank:
                   e.matmul(ps[bank][:, :], qkvb_sb[0:1, 1024 + fc * 128: 1024 + (fc + 1) * 128],
                            ones512[:], start=True, stop=False, skip_group_check=True))
                for kc in range(NKC):
                    def kf(e, kc=kc, tc=tc, fc=fc, bank=bank, last=(kc == NKC - 1)):
                        ins = e.matmul(ps[bank][:, :], wbuf[:, kc, fc * 128:(fc + 1) * 128],
                                       xT[:, kc, 1 + tc * 512: 1 + (tc + 1) * 512],
                                       start=False, stop=(kc == NKC - 1), skip_group_check=True)
                        if last:
                            ins.then_inc(cT.h, 1)
                    Te(kf)
                cT.n += 1
                wait("s", cT)
                S(lambda e, fc=fc, tc=tc, bank=bank:
                  e.activation(out=KT[:, fc, tc * 512:(tc + 1) * 512], in_=ps[bank][:, :],
                               func=AF.Copy).then_inc(cS.h, 1))
                cS.n += 1
        kt_mm_done = cT.n
        for kc in range(NKC):
            wait("g", cT, kt_mm_done)
            G(lambda e, kc=kc:
              e.dma_start(out=wbuf[:, kc, :],
                          in_=qkvw_d[kc * 128:(kc + 1) * 128, 2048:3072]).then_inc(cQG.h, 16))
            cQG.n += 16
        ldv = cQG.n
        for it in range(NT):
            for vc in range(2):
                bank = 2 + (it * 2 + vc) % 2
                wait("t", cQG, ldv)
                wait("t", cGC, qg_vpad)
                wait("t", cS)
                Te(lambda e, vc=vc, bank=bank:
                   e.matmul(ps[bank][:, :], ones1[:], qkvb_sb[0:1, 2048 + vc * 512: 2048 + (vc + 1) * 512],
                            start=True, stop=False, skip_group_check=True))
                for kc in range(NKC):
                    def vf(e, kc=kc, it=it, vc=vc, bank=bank, last=(kc == NKC - 1)):
                        ins = e.matmul(ps[bank][:, :], xT[:, kc, 1 + it * 128: 1 + (it + 1) * 128],
                                       wbuf[:, kc, vc * 512:(vc + 1) * 512],
                                       start=False, stop=(kc == NKC - 1), skip_group_check=True)
                        if last:
                            ins.then_inc(cT.h, 1)
                    Te(vf)
                cT.n += 1
                wait("s", cT)
                S(lambda e, it=it, vc=vc, bank=bank:
                  e.activation(out=vhat[:, it, vc * 8:(vc + 1) * 8, 0:64],
                               in_=ps[bank][:, :].rearrange("p (a b) -> p a b", b=64),
                               func=AF.Copy).then_inc(cS.h, 1))
                cS.n += 1
        ar.release()
        snap("snap_p1b", vhat[:], [128, NT, H, 65])
        dump("kt", KT[:], [128, NKC, T], [cS])
        dump("vhat", vhat[:], [128, NT, H, 65], [cS])

        if upto >= 6:
            ar.mark()
            Qc = ar.alloc("Qc", [128, NKC, 512], BF16)
            Pt = ar.alloc("Pt", [128, 2, 512], BF16)
            Ptmp = ar.alloc("Ptmp", [128, 128], BF16)
            rdrow = ar.alloc("rdrow", [128, 1024], F32)
            onesrow = ar.alloc("onesrow", [128, 64], F32)
            sqb = ar.alloc("sqb", [128, 1024], F32)
            y1 = ar.alloc("y1", [128, 1024], BF16)

            phase_barrier(("g",))
            G(lambda e: e.memset(onesrow[:], 1.0).then_inc(cGC.h, 1))
            cGC.n += 1
            wait("t", cGC)
            sq_done = {}
            asm_done = {}
            ynst = {}
            for c in range(4):
                for fc in range(NKC):
                    bank = fc % 2
                    wait("t", cS)
                    wait("t", cQG)
                    Te(lambda e, fc=fc, bank=bank:
                       e.matmul(ps[bank][:, :], qkvb_sb[0:1, fc * 128:(fc + 1) * 128], ones512[:],
                                start=True, stop=False, skip_group_check=True))
                    for kc in range(NKC):
                        def qf(e, kc=kc, fc=fc, c=c, bank=bank, last=(kc == NKC - 1)):
                            ins = e.matmul(ps[bank][:, :], wq_res[:, kc, fc * 128:(fc + 1) * 128],
                                           xT[:, kc, 1 + c * 512: 1 + (c + 1) * 512],
                                           start=False, stop=(kc == NKC - 1), skip_group_check=True)
                            if last:
                                ins.then_inc(cT.h, 1)
                        Te(qf)
                    cT.n += 1
                    wait("s", cT)
                    S(lambda e, fc=fc, bank=bank:
                      e.activation(out=Qc[:, fc, :], in_=ps[bank][:, :], func=AF.Copy).then_inc(cS.h, 1))
                    cS.n += 1
                qc_done = cS.n
                if c == 0:
                    snap("snap_qc", vhat[:], [128, NT, H, 65])
                if "vhatq" in dbg and c == 0:
                    d_vq = nc.dram_tensor("dbg_vhatq", [128, NT, H, 65], F32, kind="ExternalOutput").ap()
                    dbg_names.append("vhatq")
                    wait("g", cS, qc_done)
                    G(lambda e, d=d_vq: e.dma_start(out=d, in_=vhat[:]).then_inc(cQG.h, 16))
                    cQG.n += 16
                if "dbgqc0" in dbg and c == 0:
                    d_qc0 = nc.dram_tensor("dbg_qc0", [128, NKC, 512], F32, kind="ExternalOutput").ap()
                    dbg_names.append("qc0")
                    wait("g", cS, qc_done)
                    G(lambda e, d_qc0=d_qc0: e.dma_start(out=d_qc0, in_=Qc[:]).then_inc(cQG.h, 16))
                    cQG.n += 16

                hplist = [1] if "only_hp1" in skip else range(8)
                for hp in hplist:
                    if (c, hp) != (0, 0) and "only_hp1" not in skip:
                        prev = (c, hp - 1) if hp > 0 else (c - 1, 7)
                        wait("t", cS, sq_done[prev])
                        wait("t", cV, asm_done[prev])
                    last_mask = 0
                    for sub in range(2):
                        h = 2 * hp + sub
                        obank = 2 + sub
                        kchunk = h // 2
                        krow = (h % 2) * 64
                        if "mirror0" in skip and hp == 1:
                            kchunk = 0
                            h = sub
                        if "mirrorqk" in skip and hp == 1:
                            kchunk = 0
                        if "mirrorv" in skip and hp == 1:
                            h = sub
                        expd, pvd = {}, {}
                        njt = 4 * c + 4
                        if "hp1_j1" in skip and hp == 1:
                            njt = 1
                        for j in range(njt):
                            par = j % 2
                            off = max(0, (j - 4 * c) * 128)
                            wait("t", cS, qc_done)
                            if j >= 2:
                                wait("t", cS, expd[j - 2])
                            Te(lambda e, j=j, par=par, off=off, kchunk=kchunk, krow=krow:
                               e.matmul(ps[par][:, off:512],
                                        KT[krow:krow + 64, kchunk, j * 128:(j + 1) * 128],
                                        Qc[krow:krow + 64, kchunk, off:512],
                                        start=True, stop=True,
                                        skip_group_check=True).then_inc(cT.h, 1))
                            cT.n += 1
                            wait("s", cT)
                            if "dbgst" in dbg and (c, hp, sub, j) == (0, 1, 0, 1):
                                d_st = nc.dram_tensor("dbg_st", [128, 512], F32, kind="ExternalOutput").ap()
                                dbg_names.append("st")
                                stb = ar.alloc("stb", [128, 512], F32)
                                S(lambda e: e.copy(stb[:, 128:512], ps[1][:, 128:512]).then_inc(cS.h, 1))
                                cS.n += 1
                                wait("g", cS)
                                G(lambda e, d_st=d_st: e.dma_start(out=d_st, in_=stb[:]).then_inc(cQG.h, 16))
                                cQG.n += 16
                            if j >= 2:
                                wait("s", cT, pvd[j - 2])
                            diag = j >= 4 * c
                            if diag:
                                if last_mask:
                                    wait("s", cV, last_mask)
                                if off + 128 < 512:
                                    S(lambda e, par=par, off=off:
                                      e.activation(out=Ptmp[:], in_=ps[par][:, off:off + 128],
                                                   func=AF.Exp, scale=0.125))
                                    S(lambda e, par=par, off=off:
                                      e.activation(out=Pt[:, par, off + 128:512],
                                                   in_=ps[par][:, off + 128:512],
                                                   func=AF.Exp, scale=0.125).then_inc(cS.h, 1))
                                else:
                                    S(lambda e, par=par, off=off:
                                      e.activation(out=Ptmp[:], in_=ps[par][:, off:off + 128],
                                                   func=AF.Exp, scale=0.125).then_inc(cS.h, 1))
                                cS.n += 1
                                expd[j] = cS.n
                                wait("v", cS)
                                V(lambda e, par=par, off=off:
                                  e.tensor_tensor(Pt[:, par, off:off + 128], Ptmp[:], tri_sb[:],
                                                  ALU.mult).then_inc(cV.h, 1))
                                cV.n += 1
                                last_mask = cV.n
                                if "dbgpt" in dbg and (c, hp, sub, j) == (0, 1, 0, 0):
                                    d_pt = nc.dram_tensor("dbg_pt", [128, 512], F32, kind="ExternalOutput").ap()
                                    dbg_names.append("pt")
                                    wait("g", cV)
                                    G(lambda e, d_pt=d_pt, par=par: e.dma_start(out=d_pt, in_=Pt[:, par, :]).then_inc(cQG.h, 16))
                                    cQG.n += 16
                                wait("t", cV, last_mask)
                            else:
                                S(lambda e, par=par:
                                  e.activation(out=Pt[:, par, :], in_=ps[par][:, :],
                                               func=AF.Exp, scale=0.125).then_inc(cS.h, 1))
                                cS.n += 1
                                expd[j] = cS.n
                                wait("t", cS, expd[j])
                            Te(lambda e, j=j, h=h, par=par, off=off, obank=obank, njt=njt:
                               e.matmul(ps[obank][0:65, off:512], vhat[:, j, h, :],
                                        Pt[:, par, off:512],
                                        start=(j == 0), stop=(j == njt - 1),
                                        skip_group_check=True).then_inc(cT.h, 1))
                            cT.n += 1
                            pvd[j] = cT.n
                    wait("s", cT)
                    if "dbgov" in dbg and (c, hp) == (0, 1):
                        d_ov = nc.dram_tensor("dbg_ov", [128, 512], F32, kind="ExternalOutput").ap()
                        dbg_names.append("ov")
                        ovb = ar.alloc("ovb", [128, 512], F32)
                        S(lambda e: e.copy(ovb[0:64, :], ps[2][0:64, :]))
                        S(lambda e: e.copy(ovb[64:65, :], ps[2][64:65, :]).then_inc(cS.h, 1))
                        cS.n += 1
                        wait("g", cS)
                        G(lambda e, d_ov=d_ov: e.dma_start(out=d_ov, in_=ovb[:]).then_inc(cQG.h, 16))
                        cQG.n += 16
                    wait("v", cT)
                    if (c, hp) != (0, 0) and "only_hp1" not in skip:
                        prev = (c, hp - 1) if hp > 0 else (c - 1, 7)
                        wait("s", cV, asm_done[prev])
                        wait("v", cV, asm_done[prev])
                    V(lambda e: e.reciprocal(rdrow[64:65, 0:512], ps[2][64:65, :]))
                    V(lambda e: e.reciprocal(rdrow[64:65, 512:1024], ps[3][64:65, :]).then_inc(cV.h, 1))
                    cV.n += 1
                    S(lambda e: e.copy(sqb[0:64, 0:512], ps[2][0:64, :]))
                    S(lambda e: e.copy(sqb[0:64, 512:1024], ps[3][0:64, :]).then_inc(cS.h, 1))
                    cS.n += 1
                    sq_done[(c, hp)] = cS.n
                    wait("t", cV)
                    Te(lambda e: e.matmul(ps[4][0:64, :], onesrow[64:65, :], rdrow[64:65, 0:512],
                                          start=True, stop=True, skip_group_check=True))
                    Te(lambda e: e.matmul(ps[5][0:64, :], onesrow[64:65, :], rdrow[64:65, 512:1024],
                                          start=True, stop=True,
                                          skip_group_check=True).then_inc(cT.h, 1))
                    cT.n += 1
                    wait("v", cT)
                    wait("v", cS)
                    if (c, hp) in ynst:
                        wait("v", cQS, ynst[(c, hp)])
                    V(lambda e: e.tensor_tensor(y1[0:64, 0:512], sqb[0:64, 0:512], ps[4][0:64, :], ALU.mult))
                    V(lambda e: e.tensor_tensor(y1[0:64, 512:1024], sqb[0:64, 512:1024],
                                                ps[5][0:64, :], ALU.mult).then_inc(cV.h, 1))
                    cV.n += 1
                    asm_done[(c, hp)] = cV.n
                    if False and "dbgsq1" in dbg and (c, hp) == (0, 1):
                        d_sq1 = nc.dram_tensor("dbg_sq1", [128, 1024], F32, kind="ExternalOutput").ap()
                        d_rd1 = nc.dram_tensor("dbg_rd1", [128, 1024], F32, kind="ExternalOutput").ap()
                        dbg_names.extend(["sq1", "rd1"])
                        wait("g", cS)
                        wait("g", cV)
                        G(lambda e, d=d_sq1: e.dma_start(out=d, in_=sqb[:]).then_inc(cQG.h, 16))
                        cQG.n += 16
                        G(lambda e, d=d_rd1: e.dma_start(out=d, in_=rdrow[:]).then_inc(cQG.h, 16))
                        cQG.n += 16
                    wait("y", cV)
                    Y(lambda e, hp=hp, c=c:
                      e.dma_start(out=yn_dram[hp * 128: hp * 128 + 64, c * 512:(c + 1) * 512],
                                  in_=y1[0:64, 0:512]).then_inc(cQS.h, 16))
                    cQS.n += 16
                    Y(lambda e, hp=hp, c=c:
                      e.dma_start(out=yn_dram[hp * 128 + 64: hp * 128 + 128, c * 512:(c + 1) * 512],
                                  in_=y1[0:64, 512:1024]).then_inc(cQS.h, 16))
                    cQS.n += 16
                    nxt = (c, hp + 1) if hp < 7 else (c + 1, 0)
                    ynst[nxt] = cQS.n
                    if (c, hp) == (0, 0):
                        snap("snap_h0", vhat[:], [128, NT, H, 65])
                    if (c, hp) == (0, 1):
                        snap("snap_h1", vhat[:], [128, NT, H, 65])
                    if False and "vhath0" in dbg and (c, hp) == (0, 0):
                        d_vh0 = nc.dram_tensor("dbg_vhath0", [128, NT, H, 65], F32, kind="ExternalOutput").ap()
                        dbg_names.append("vhath0")
                        wait("g", cV)
                        wait("g", cS)
                        G(lambda e, d=d_vh0: e.dma_start(out=d, in_=vhat[:]).then_inc(cQG.h, 16))
                        cQG.n += 16
            dump("qc", Qc[:], [128, NKC, 512], [cS, cT])
            dump("vhat2", vhat[:], [128, NT, H, 65], [cS, cT, cV])
            dump("onesrow", onesrow[:], [128, 64], [cV, cT])
            dump("rdrow", rdrow[:], [128, 1024], [cV, cT])
            dump("sqb", sqb[:], [128, 1024], [cS])
            ar.release()
            if "yn" in dbg:
                d = nc.dram_tensor("dbg_yn", [D, T], F32, kind="ExternalOutput").ap()
                dbg_names.append("yn")
                wait("g", cQG)
                wait("g", cQS)
                G(lambda e, d=d: e.dma_start(out=d, in_=yn_dram[:, :]).then_inc(cQG.h, 16))
                cQG.n += 16
        ar.release()

    if upto >= 7:
        ar.mark()
        memvw_sb = ar.alloc("memvw_sb", [128, NKC, D], BF16)
        outw_sb = ar.alloc("outw_sb", [128, NKC, D], BF16)
        ync = ar.alloc("ync", [128, NKC, 512], BF16)
        yfin = ar.alloc("yfin", [128, NKC, 512], BF16)
        gbsb = ar.alloc("gbsb", [128, 512], F32)
        y2 = ar.alloc("y2", [128, 512], F32)
        xrl = ar.alloc("xrl", [128, 2, D], F32)
        rbuf = ar.alloc("rbuf", [128, D], F32)
        x2buf = ar.alloc("x2buf", [128, D], F32)
        bst = ar.alloc("bst", [128, 12], F32)
        mv2 = ar.alloc("mv2", [128, 2], F32)
        veps = ar.alloc("veps", [128, 1], F32)
        sdv = ar.alloc("sdv", [128, 1], F32)
        rstd = ar.alloc("rstd", [128, 1], F32)
        mup = ar.alloc("mup", [128, 1], F32)

        phase_barrier()
        G(lambda e: e.dma_start(out=memvw_sb[:],
                                in_=memvw_d[:].rearrange("(k p) f -> p k f", p=128)).then_inc(cQG.h, 16))
        cQG.n += 16
        G(lambda e: e.dma_start(out=outw_sb[:],
                                in_=outw_d[:].rearrange("(k p) f -> p k f", p=128)).then_inc(cQG.h, 16))
        cQG.n += 16
        ld4 = cQG.n
        p4v = {}
        for c in range(4):
            wait("y", cV)
            Y(lambda e, c=c:
              e.dma_start(out=ync[:], in_=yn_dram[:, c * 512:(c + 1) * 512]
                          .rearrange("(k p) t -> p k t", p=128)).then_inc(cQS.h, 16))
            cQS.n += 16
            wait("t", cS)
            wait("t", cQG, ld4)
            Te(lambda e, c=c: e.matmul(ps[5][:, :], ones1[:], grow[0:1, c * 512:(c + 1) * 512],
                                       start=True, stop=True,
                                       skip_group_check=True).then_inc(cT.h, 1))
            cT.n += 1
            wait("s", cT)
            S(lambda e: e.activation(out=gbsb[:], in_=ps[5][:, :], func=AF.Copy).then_inc(cS.h, 1))
            cS.n += 1
            ldyn = cQS.n
            for hp in range(NKC):
                if p4v:
                    wait("t", cV, max(p4v.values()))
                Te(lambda e, hp=hp:
                   e.matmul(ps[6][:, :], memvb_sb[0:1, hp * 128:(hp + 1) * 128], ones512[:],
                            start=True, stop=False, skip_group_check=True))
                for kc in range(NKC):
                    def mvf(e, kc=kc, hp=hp, c=c, last=(kc == NKC - 1)):
                        ins = e.matmul(ps[6][:, :], memvw_sb[:, kc, hp * 128:(hp + 1) * 128],
                                       xT[:, kc, 1 + c * 512: 1 + (c + 1) * 512],
                                       start=False, stop=(kc == NKC - 1), skip_group_check=True)
                        if last:
                            ins.then_inc(cT.h, 1)
                    Te(mvf)
                cT.n += 1
                wait("v", cT)
                wait("v", cS)
                wait("v", cQS, ldyn)
                V(lambda e: e.tensor_tensor(y2[:], gbsb[:], ps[6][:, :], ALU.mult))
                V(lambda e, hp=hp:
                  e.tensor_tensor(yfin[:, hp, :], ync[:, hp, :], y2[:], ALU.add).then_inc(cV.h, 1))
                cV.n += 1
                p4v[hp] = cV.n
            for ti in range(4):
                it = c * 4 + ti
                wait("y", cV)
                Y(lambda e, it=it, ti=ti:
                  e.dma_start(out=xrl[:, ti % 2, :], in_=x_d[it * 128:(it + 1) * 128, :]).then_inc(cQS.h, 16))
                cQS.n += 16
                ldx = cQS.n
                wait("t", cV, max(p4v.values()))
                wait("t", cV)
                for half in range(2):
                    bank = half
                    Te(lambda e, half=half, bank=bank:
                       e.matmul(ps[bank][:, :], ones1[:], outb_sb[0:1, half * 512:(half + 1) * 512],
                                start=True, stop=False, skip_group_check=True))
                    for kc in range(NKC):
                        def of(e, kc=kc, ti=ti, half=half, bank=bank,
                               last=(half == 1 and kc == NKC - 1)):
                            ins = e.matmul(ps[bank][:, :], yfin[:, kc, ti * 128:(ti + 1) * 128],
                                           outw_sb[:, kc, half * 512:(half + 1) * 512],
                                           start=False, stop=(kc == NKC - 1), skip_group_check=True)
                            if last:
                                ins.then_inc(cT.h, 1)
                        Te(of)
                cT.n += 1
                wait("v", cT)
                wait("v", cQS, ldx)
                wait("v", cS)
                wait("v", cQG)
                V(lambda e, ti=ti: e.tensor_tensor(rbuf[:, 0:512], xrl[:, ti % 2, 0:512], ps[0][:, :], ALU.add))
                V(lambda e, ti=ti: e.tensor_tensor(rbuf[:, 512:1024], xrl[:, ti % 2, 512:1024], ps[1][:, :], ALU.add).then_inc(cV.h, 1))
                cV.n += 1
                ln_tile(rbuf[:], x2buf[:], bst, mv2, veps, sdv, rstd, mup)
                wait("g", cS)
                G(lambda e, it=it:
                  e.dma_start(out=x2_dram[it * 128:(it + 1) * 128, :], in_=x2buf[:]).then_inc(cQG.h, 16))
                cQG.n += 16
        ar.release()
        if "x2" in dbg:
            d_x2 = nc.dram_tensor("dbg_x2", [T, D], F32, kind="ExternalOutput").ap()
            dbg_names.append("x2")
            wait("g", cQG)
            wait("g", cS)
            G(lambda e, d=d_x2: e.dma_start(out=d, in_=x2_dram[:, :]).then_inc(cQG.h, 16))
            cQG.n += 16

    if upto >= 8:
        ar.mark()
        xq_in = ar.alloc("xq_in", [128, 2, D], F32)
        xq_out = ar.alloc("xq_out", [128, 2, D], BF16)
        fA = ar.alloc("fA", [128, D], F32)
        fB = ar.alloc("fB", [128, D], F32)
        fC = ar.alloc("fC", [128, D], F32)
        fD = ar.alloc("fD", [128, D], F32)
        fmx = ar.alloc("fmx", [128, 64], F32)
        finv = ar.alloc("finv", [128, 64], F32)
        fhs = ar.alloc("fhs", [128, 64], F32)
        phase_barrier()
        xq_q, xq_s = {}, {}
        for it in range(NT):
            par = it % 2
            if it >= 2:
                wait("y", cV, xq_q[it - 2])
            Y(lambda e, it=it, par=par:
              e.dma_start(out=xq_in[:, par, :], in_=x2_dram[it * 128:(it + 1) * 128, :]).then_inc(cQS.h, 16))
            cQS.n += 16
            wait("v", cQS)
            if it >= 2:
                wait("v", cQC, xq_s[it - 2])
            quant_chain(xq_in[:, par, :], D, fA, fB, fC, fD, fmx, finv, fhs,
                        xq_out[:, par, :].rearrange("p (a b) -> p a b", b=16))
            xq_q[it] = cV.n
            wait("s", cV)
            S(lambda e, it=it, par=par:
              e.dma_start(out=x2q_dram[it * 128:(it + 1) * 128, :], in_=xq_out[:, par, :]).then_inc(cQC.h, 16))
            cQC.n += 16
            xq_s[it] = cQC.n
        ar.release()

        ar.mark()
        hq_tm = ar.alloc("hq_tm", [128, NT, FF], BF16)
        ar.mark()
        x2qT = ar.alloc("x2qT", [128, NKC, T], BF16)
        wslc = ar.alloc("wslc", [128, 2, NKC, 512], BF16)
        gsil = ar.alloc("gsil", [128, 512], F32)
        hbuf = ar.alloc("hbuf", [128, 512], F32)
        gA = ar.alloc("gA", [128, 512], F32)
        gB = ar.alloc("gB", [128, 512], F32)
        gC = ar.alloc("gC", [128, 512], F32)
        gD = ar.alloc("gD", [128, 512], F32)
        gmx = ar.alloc("gmx", [128, 32], F32)
        ginv = ar.alloc("ginv", [128, 32], F32)
        ghs = ar.alloc("ghs", [128, 32], F32)

        phase_barrier()
        wait("y", cQC)
        for kc in range(NKC):
            Y(lambda e, kc=kc:
              e.dma_start(out=x2qT[:, kc, :], in_=x2q_dram[:, kc * 128:(kc + 1) * 128],
                          transpose=True).then_inc(cQS.h, 16))
            cQS.n += 16
        ldxq = cQS.n
        for ici, (ic0, icw) in enumerate(IC_CHUNKS):
            wait("y", cV)
            wait("y", cQG)
            for wi, wsrc in enumerate([wgq_dram, wuq_dram]):
                for kc in range(NKC):
                    Y(lambda e, wi=wi, wsrc=wsrc, kc=kc, ic0=ic0, icw=icw:
                      e.dma_start(out=wslc[:, wi, kc, 0:icw],
                                  in_=wsrc[ic0:ic0 + icw, kc * 128:(kc + 1) * 128],
                                  transpose=True).then_inc(cQS.h, 16))
                    cQS.n += 16
            ldw = cQS.n
            for it in range(NT):
                gidx5 = ici * NT + it
                gb_ = (it % 2) * 2
                ub_ = gb_ + 1
                wait("t", cQS, max(ldw, ldxq))
                if gidx5 >= 2:
                    wait("t", cV, hq_done[gidx5 - 2])
                for wi, bank in [(0, gb_), (1, ub_)]:
                    for kc in range(NKC):
                        def guf(e, wi=wi, kc=kc, it=it, icw=icw, bank=bank,
                                last=(wi == 1 and kc == NKC - 1)):
                            ins = e.matmul(ps[bank][:, 0:icw], x2qT[:, kc, it * 128:(it + 1) * 128],
                                           wslc[:, wi, kc, 0:icw],
                                           start=(kc == 0), stop=(kc == NKC - 1), skip_group_check=True)
                            if last:
                                ins.then_inc(cT.h, 1)
                        Te(guf)
                cT.n += 1
                wait("s", cT)
                wait("s", cV)
                S(lambda e, gb_=gb_, icw=icw:
                  e.activation(out=gsil[:, 0:icw], in_=ps[gb_][:, 0:icw], func=AF.Silu).then_inc(cS.h, 1))
                cS.n += 1
                wait("v", cS)
                V(lambda e, ub_=ub_, icw=icw:
                  e.tensor_tensor(hbuf[:, 0:icw], gsil[:, 0:icw], ps[ub_][:, 0:icw], ALU.mult).then_inc(cV.h, 1))
                cV.n += 1
                quant_chain(hbuf[:, 0:icw], icw, gA, gB, gC, gD, gmx, ginv, ghs,
                            hq_tm[:, it, ic0:ic0 + icw].rearrange("p (a b) -> p a b", b=16))
                hq_done = hq_done if gidx5 > 0 else {}
                hq_done[gidx5] = cV.n
        ar.release()

        wait("g", cV)
        for it in range(NT):
            G(lambda e, it=it:
              e.dma_start(out=hq_dram[it * 128:(it + 1) * 128, :], in_=hq_tm[:, it, :]).then_inc(cQG.h, 16))
            cQG.n += 16
        hq_st = cQG.n
        ar.release()

        ar.mark()
        wdqT = ar.alloc("wdqT", [128, 22, D], BF16)
        hqT = ar.alloc("hqT", [128, 2, 512], BF16)
        xr2 = ar.alloc("xr2", [128, 2, D], F32)
        rb2 = ar.alloc("rb2", [128, 2, D], F32)
        ob2 = ar.alloc("ob2", [128, 2, D], F32)
        bst2 = ar.alloc("bst2", [128, 12], F32)
        mv22 = ar.alloc("mv22", [128, 2], F32)
        veps2 = ar.alloc("veps2", [128, 1], F32)
        sdv2 = ar.alloc("sdv2", [128, 1], F32)
        rstd2 = ar.alloc("rstd2", [128, 1], F32)
        mup2 = ar.alloc("mup2", [128, 1], F32)

        phase_barrier()
        for ic in range(22):
            Y(lambda e, ic=ic:
              e.dma_start(out=wdqT[:, ic, :], in_=wdq_dram[:, ic * 128:(ic + 1) * 128],
                          transpose=True).then_inc(cQS.h, 16))
            cQS.n += 16
        ldwd = cQS.n
        for tg in range(4):
            hq_ld = {}
            for ic in range(22):
                gidx6 = tg * 22 + ic
                par = ic % 2
                if gidx6 >= 2:
                    wait("y", cT, hq_mm[gidx6 - 2])
                Y(lambda e, tg=tg, ic=ic, par=par:
                  e.dma_start(out=hqT[:, par, :], in_=hq_dram[tg * 512:(tg + 1) * 512, ic * 128:(ic + 1) * 128],
                              transpose=True).then_inc(cQS.h, 16))
                cQS.n += 16
                hq_ld[ic] = cQS.n
                wait("t", cQS, max(hq_ld[ic], ldwd))
                wait("t", cQG, hq_st)
                if ic == 0:
                    wait("t", cS)
                    wait("t", cV)
                for ti in range(4):
                    for half in range(2):
                        bank = ti * 2 + half
                        def df(e, ic=ic, ti=ti, half=half, par=par, bank=bank,
                               last=(ti == 3 and half == 1)):
                            ins = e.matmul(ps[bank][:, :], hqT[:, par, ti * 128:(ti + 1) * 128],
                                           wdqT[:, ic, half * 512:(half + 1) * 512],
                                           start=(ic == 0), stop=(ic == 21), skip_group_check=True)
                            if last:
                                ins.then_inc(cT.h, 1)
                        Te(df)
                cT.n += 1
                hq_mm = hq_mm if gidx6 > 0 else {}
                hq_mm[gidx6] = cT.n
            dn_done = cT.n
            for ti in range(4):
                it = tg * 4 + ti
                par = ti % 2
                if it >= 2:
                    wait("y", cV, add_done[it - 2])
                Y(lambda e, it=it, par=par:
                  e.dma_start(out=xr2[:, par, :], in_=x2_dram[it * 128:(it + 1) * 128, :]).then_inc(cQS.h, 16))
                cQS.n += 16
                ldx2 = cQS.n
                wait("v", cT, dn_done)
                wait("v", cQS, ldx2)
                wait("v", cS)
                wait("v", cQG)
                V(lambda e, ti=ti, par=par:
                  e.tensor_tensor(rb2[:, par, 0:512], xr2[:, par, 0:512], ps[ti * 2][:, :], ALU.add))
                V(lambda e, ti=ti, par=par:
                  e.tensor_tensor(rb2[:, par, 512:1024], xr2[:, par, 512:1024], ps[ti * 2 + 1][:, :], ALU.add).then_inc(cV.h, 1))
                cV.n += 1
                add_done = add_done if it > 0 else {}
                add_done[it] = cV.n
                ln_tile(rb2[:, par, :], ob2[:, par, :], bst2, mv22, veps2, sdv2, rstd2, mup2)
                wait("g", cS)
                G(lambda e, it=it, par=par:
                  e.dma_start(out=out_d[it * 128:(it + 1) * 128, :], in_=ob2[:, par, :]).then_inc(cQG.h, 16))
                cQG.n += 16
        ar.release()

    if upto >= 8:
        for nm, dr, shp in (("wgq", wgq_dram, [FF, D]), ("wuq", wuq_dram, [FF, D]),
                            ("wdq", wdq_dram, [D, FF]), ("x2f", x2_dram, [T, D]),
                            ("x2q", x2q_dram, [T, D]), ("hqd", hq_dram, [T, FF])):
            if nm in dbg:
                dd = nc.dram_tensor("dbg_" + nm, shp, F32, kind="ExternalOutput").ap()
                dbg_names.append(nm)
                wait("g", cQG)
                wait("g", cQC)
                wait("g", cS)
                wait("g", cV)
                G(lambda e, dd=dd, dr=dr: e.dma_start(out=dd, in_=dr[:, :]).then_inc(cQG.h, 16))
                cQG.n += 16

    wait("y", cQG)
    wait("y", cQS)
    wait("y", cQC)
    wait("y", cT)
    wait("y", cV)
    wait("y", cS)

    backing = ctx.enter_context(nc.sbuf_tensor("_arena", [128, ((ar.peak + 31) // 32) * 32], U8))
    ar.finalize(backing)
    with nc.Block() as block:
        @block.tensor
        def _(e):
            for f in ops['t']: f(e)
        @block.vector
        def _(e):
            for f in ops['v']: f(e)
        @block.scalar
        def _(e):
            for f in ops['s']: f(e)
        @block.gpsimd
        def _(e):
            for f in ops['g']: f(e)
        @block.sync
        def _(e):
            for f in ops['y']: f(e)
    ctx.close()
    nc._ar = ar
    return nc, dbg_names


def make_inmap(inputs, b):
    ident, tri, esel, e2 = host_consts()
    m = {
        "x": inputs["L0_attn_x"][b],
        "qkv_w": inputs["L0_attn_qkv_w"],
        "qkv_b": inputs["L0_attn_qkv_b"].reshape(1, -1),
        "w1w": inputs["L0_attn_w1w"],
        "w2w": inputs["L0_attn_w2w"],
        "r1w": inputs["L0_attn_r1w"],
        "r2w": inputs["L0_attn_r2w"],
        "memv_w": inputs["L0_attn_memv_w"],
        "memv_b": inputs["L0_attn_memv_b"].reshape(1, -1),
        "memg_w": inputs["L0_attn_memg_w"],
        "memg_b": inputs["L0_attn_memg_b"].reshape(1, -1),
        "mem_scale": inputs["L0_attn_mem_scale"].reshape(16, 1),
        "out_w": inputs["L0_attn_out_w"],
        "out_b": inputs["L0_attn_out_b"].reshape(1, -1),
        "w_gate": inputs["L0_ffn_w_gate"],
        "w_up": inputs["L0_ffn_w_up"],
        "w_down": inputs["L0_ffn_w_down"],
        "c_ident": ident, "c_tri": tri, "c_esel": esel, "c_e2": e2,
    }
    return {k: np.ascontiguousarray(np.asarray(v, dtype=np.float32)) for k, v in m.items()}


def kernel(**inputs):
    nc, _ = build_program()
    in_maps = [make_inmap(inputs, b) for b in range(8)]
    res = run_bass_kernel_spmd(nc, in_maps, core_ids=list(range(8)))
    return np.stack([np.asarray(res.results[b]["out"], np.float32) for b in range(8)], axis=0)


# version 02
# speedup vs baseline: 1.0809x; 1.0809x over previous
import contextlib
import numpy as np
import ml_dtypes
import concourse.bass as bass
import concourse.mybir as mybir
from concourse.bass_utils import run_bass_kernel_spmd

F32 = mybir.dt.float32
BF16 = mybir.dt.bfloat16
U8 = mybir.dt.uint8
AF = mybir.ActivationFunctionType
ALU = mybir.AluOpType
AX = mybir.AxisListType

T, D, H, PD, DH = 2048, 1024, 16, 4, 64
FF = 2816
NT = T // 128
NKC = D // 128
DECAY = 0.99
MAGIC = 12582912.0
EPS = 1e-5

PAIRS4 = [(0, 1), (0, 2), (0, 3), (1, 2), (1, 3), (2, 3)]
JW_PERM = [5, 4, 3, 2, 1, 0]
JW_SIGN = [1.0, -1.0, 1.0, 1.0, -1.0, 1.0]
IJ_PAIRS = [(a, b) for a in range(6) for b in range(a, 6)]
IC_CHUNKS = [(0, 512), (512, 512), (1024, 512), (1536, 512), (2048, 512), (2560, 256)]


def host_consts():
    ident = np.eye(128, dtype=np.float32)
    col = np.arange(128)[None, :]
    row = np.arange(128)[:, None]
    tri = (col >= row).astype(np.float32)
    esel = np.zeros((112, 16), np.float32)
    for h in range(16):
        esel[h * 7:(h + 1) * 7, h] = 1.0
    e2 = np.zeros((2, 128), np.float32)
    e2[0, :64] = 1.0
    e2[1, 64:] = 1.0
    return ident, tri, esel, e2


class Arena:
    def __init__(self, nc):
        self.nc = nc
        self.cur = 0
        self.peak = 0
        self.items = []
        self.marks = []

    def alloc(self, name, shape, dtype):
        per = int(np.prod(shape[1:])) * mybir.dt.size(dtype)
        addr = (self.cur + 31) // 32 * 32
        self.cur = addr + per
        self.peak = max(self.peak, self.cur)
        h = self.nc.virtual_sbuf_tensor(name, shape, dtype)
        self.items.append((h, addr, shape, dtype))
        return h

    def mark(self):
        self.marks.append(self.cur)

    def release(self):
        self.cur = self.marks.pop()

    def finalize(self, backing):
        off = self.nc.lookup_mloc(backing).addr
        for h, addr, shape, dtype in self.items:
            self.nc._tensor(h.name, list(shape), dtype, type="SB")
            mloc = self.nc.lookup_mloc(h)
            mloc.allocated = not self.nc.target_bir_lowering
            mloc.addr = off + addr


class Ctr:
    def __init__(self, h):
        self.h = h
        self.n = 0


def build_program(dbg=(), upto=99, skip=(), detect_races=True):
    nc = bass.Bass("TRN2", target_bir_lowering=False, detect_race_conditions=detect_races)

    def din(name, shape, dtype=F32):
        return nc.dram_tensor(name, list(shape), dtype, kind="ExternalInput").ap()

    x_d = din("x", [T, D])
    qkvw_d = din("qkv_w", [D, 3 * D])
    qkvb_d = din("qkv_b", [1, 3 * D])
    w1w_d = din("w1w", [D, 64])
    w2w_d = din("w2w", [D, 64])
    r1w_d = din("r1w", [D, 64])
    r2w_d = din("r2w", [D, 64])
    memvw_d = din("memv_w", [D, D])
    memvb_d = din("memv_b", [1, D])
    memgw_d = din("memg_w", [D, 16])
    memgb_d = din("memg_b", [1, 16])
    mscale_d = din("mem_scale", [16, 1])
    outw_d = din("out_w", [D, D])
    outb_d = din("out_b", [1, D])
    wg_d = din("w_gate", [FF, D])
    wu_d = din("w_up", [FF, D])
    wd_d = din("w_down", [D, FF])
    ident_d = din("c_ident", [128, 128])
    tri_d = din("c_tri", [128, 128])
    esel_d = din("c_esel", [112, 16])
    e2_d = din("c_e2", [2, 128])

    out_d = nc.dram_tensor("out", [T, D], F32, kind="ExternalOutput").ap()

    x2_dram = nc.dram_tensor("x2_scr", [T, D], F32, kind="Internal").ap()
    x2q_dram = nc.dram_tensor("x2q_scr", [T, D], BF16, kind="Internal").ap()
    yn_dram = nc.dram_tensor("yn_scr", [D, T], BF16, kind="Internal").ap()
    hq_dram = nc.dram_tensor("hq_scr", [T, FF], BF16, kind="Internal").ap()
    wgq_dram = nc.dram_tensor("wgq_scr", [FF, D], BF16, kind="Internal").ap()
    wuq_dram = nc.dram_tensor("wuq_scr", [FF, D], BF16, kind="Internal").ap()
    wdq_dram = nc.dram_tensor("wdq_scr", [D, FF], BF16, kind="Internal").ap()

    ops = {k: [] for k in "tvsgy"}
    def Te(f): ops['t'].append(f)
    def V(f): ops['v'].append(f)
    def S(f): ops['s'].append(f)
    def G(f): ops['g'].append(f)
    def Y(f): ops['y'].append(f)
    ENG = {"t": Te, "v": V, "s": S, "g": G, "y": Y}

    ctx = contextlib.ExitStack()
    ps = [ctx.enter_context(nc.psum_tensor(f"ps{i}", [128, 512], F32)) for i in range(8)]
    sem_names = ["qs", "qg", "qc", "t", "v", "s", "gc"]
    sems = {n: ctx.enter_context(nc.semaphore("sem_" + n)) for n in sem_names}
    cQS, cQG, cQC, cT, cV, cS, cGC = (Ctr(sems[n]) for n in sem_names)

    def wait(engkey, c, n=None):
        n = c.n if n is None else n
        if n > 0:
            ENG[engkey](lambda e, h=c.h, n=n: e.wait_ge(h, n))

    ar = Arena(nc)
    dbg_names = []

    def phase_barrier(engs=("y", "g")):
        for ek in engs:
            wait(ek, cT)
            wait(ek, cV)
            wait(ek, cS)
            wait(ek, cQG)
            wait(ek, cQS)
            wait(ek, cGC)

    def snap(name, src_ap, shape):
        if name not in dbg:
            return
        d = nc.dram_tensor("dbg_" + name, list(shape), F32, kind="ExternalOutput").ap()
        dbg_names.append(name)
        for ek in ("g",):
            wait(ek, cT)
            wait(ek, cV)
            wait(ek, cS)
            wait(ek, cQS)
        G(lambda e, d=d, src=src_ap: e.dma_start(out=d, in_=src).then_inc(cQG.h, 16))
        cQG.n += 16
        for ek in ("t", "v", "s", "y"):
            wait(ek, cQG)

    def dump(name, src_ap, shape, wait_ctrs):
        if name not in dbg:
            return
        d = nc.dram_tensor("dbg_" + name, list(shape), F32, kind="ExternalOutput").ap()
        dbg_names.append(name)
        for c in wait_ctrs:
            wait("g", c)
        G(lambda e, d=d, src=src_ap: e.dma_start(out=d, in_=src).then_inc(cQG.h, 16))
        cQG.n += 16

    ident_sb = ar.alloc("ident_sb", [128, 128], F32)
    tri_sb = ar.alloc("tri_sb", [128, 128], F32)
    esel_sb = ar.alloc("esel_sb", [112, 16], F32)
    e2_sb = ar.alloc("e2_sb", [2, 128], F32)
    ones1 = ar.alloc("ones1", [1, 128], F32)
    ones512 = ar.alloc("ones512", [1, 512], F32)
    sixt = ar.alloc("sixt", [16, 1], F32)
    mscale_sb = ar.alloc("mscale_sb", [16, 1], F32)
    memvb_sb = ar.alloc("memvb_sb", [1, D], F32)
    outb_sb = ar.alloc("outb_sb", [1, D], F32)
    memgb_sb = ar.alloc("memgb_sb", [1, 16], F32)

    for dst, src in [(ident_sb, ident_d), (tri_sb, tri_d), (esel_sb, esel_d),
                     (e2_sb, e2_d), (mscale_sb, mscale_d),
                     (memvb_sb, memvb_d), (outb_sb, outb_d), (memgb_sb, memgb_d)]:
        Y(lambda e, dst=dst, src=src: e.dma_start(out=dst[:], in_=src).then_inc(cQS.h, 16))
        cQS.n += 16
    G(lambda e: e.memset(ones1[:], 1.0))
    G(lambda e: e.memset(ones512[:], 1.0))
    G(lambda e: e.memset(sixt[:], 1.0 / 16.0).then_inc(cGC.h, 1))
    cGC.n += 1

    def quant_chain(src2d, n, A, Bt, C, Dt, mx, inv, hs, out3d, inc_after=True):
        nb = n // 16
        s3 = src2d.rearrange("p (a b) -> p a b", b=16)
        def mxb(t_):
            return t_[:, 0:nb].unsqueeze(2).broadcast_to([128, nb, 16])
        V(lambda e: e.tensor_reduce(mx[:, 0:nb], s3, AX.X, ALU.max,
                                    apply_absolute_value=True).then_inc(cV.h, 1))
        cV.n += 1
        wait("v", cV)
        V(lambda e: e.tensor_scalar(hs[:, 0:nb], mx[:, 0:nb], 1.0 / 6.0, 1e-12,
                                    ALU.mult, ALU.max).then_inc(cV.h, 1))
        cV.n += 1
        wait("v", cV)
        V(lambda e: e.reciprocal(inv[:, 0:nb], hs[:, 0:nb]).then_inc(cV.h, 1))
        cV.n += 1
        wait("v", cV)
        B3 = Bt[:, 0:n].rearrange("p (a b) -> p a b", b=16)
        V(lambda e: e.tensor_tensor(B3, s3, mxb(inv), ALU.mult))
        V(lambda e: e.tensor_scalar(A[:, 0:n], Bt[:, 0:n], 2.0, -2.0, ALU.min, ALU.max))
        V(lambda e: e.tensor_scalar(C[:, 0:n], Bt[:, 0:n], 0.5, 2.0, ALU.mult, ALU.min))
        V(lambda e: e.scalar_tensor_tensor(Dt[:, 0:n], C[:, 0:n], -2.0, A[:, 0:n], ALU.max, ALU.add))
        V(lambda e: e.scalar_tensor_tensor(C[:, 0:n], Bt[:, 0:n], 0.5, Dt[:, 0:n], ALU.mult, ALU.add))
        V(lambda e: e.tensor_scalar(Bt[:, 0:n], C[:, 0:n], MAGIC, MAGIC, ALU.add, ALU.subtract))
        V(lambda e: e.tensor_scalar(A[:, 0:n], Bt[:, 0:n], 4.0, -4.0, ALU.min, ALU.max))
        V(lambda e: e.tensor_scalar(C[:, 0:n], Bt[:, 0:n], 6.0, -6.0, ALU.min, ALU.max))
        V(lambda e: e.scalar_tensor_tensor(Dt[:, 0:n], Bt[:, 0:n], 2.0, C[:, 0:n], ALU.mult, ALU.subtract))
        V(lambda e: e.scalar_tensor_tensor(C[:, 0:n], A[:, 0:n], -0.5, Dt[:, 0:n], ALU.mult, ALU.add))
        C3 = C[:, 0:n].rearrange("p (a b) -> p a b", b=16)
        if inc_after:
            V(lambda e: e.tensor_tensor(out3d, C3, mxb(hs), ALU.mult).then_inc(cV.h, 1))
            cV.n += 1
        else:
            V(lambda e: e.tensor_tensor(out3d, C3, mxb(hs), ALU.mult))

    def ln_tile(r_ap, out_ap, bst, mv2, veps, sdv, rstd, mup):
        V(lambda e: e.bn_stats(bst[:, 0:6], r_ap[:, 0:512]))
        V(lambda e: e.bn_stats(bst[:, 6:12], r_ap[:, 512:1024]).then_inc(cV.h, 1))
        cV.n += 1
        wait("v", cV)
        V(lambda e: e.bn_aggr(mv2[:], bst[:]).then_inc(cV.h, 1))
        cV.n += 1
        wait("v", cV)
        V(lambda e: e.tensor_scalar(veps[:], mv2[:, 1:2], EPS, None, ALU.add).then_inc(cV.h, 1))
        cV.n += 1
        wait("s", cV)
        S(lambda e: e.activation(out=sdv[:], in_=veps[:], func=AF.Sqrt).then_inc(cS.h, 1))
        cS.n += 1
        wait("v", cS)
        V(lambda e: e.reciprocal(rstd[:], sdv[:]).then_inc(cV.h, 1))
        cV.n += 1
        wait("v", cV)
        V(lambda e: e.tensor_scalar(mup[:], mv2[:, 0:1], rstd[:, 0:1], -1.0, ALU.mult, ALU.mult).then_inc(cV.h, 1))
        cV.n += 1
        wait("s", cV)
        S(lambda e: e.activation(out=out_ap, in_=r_ap, func=AF.Identity,
                                 bias=mup[:], scale=rstd[:]).then_inc(cS.h, 1))
        cS.n += 1

    xT = ar.alloc("xT", [128, NKC, T + 1], BF16)
    grow = ar.alloc("grow", [1, T], F32)
    G(lambda e: e.memset(xT[:, :, 0:1], 0.0).then_inc(cGC.h, 1))
    cGC.n += 1
    qg_pads = cGC.n

    ar.mark()
    pwf = ar.alloc("pwf", [128, 2, 1024], F32)
    pwq = ar.alloc("pwq", [128, 2, 1024], BF16)
    pA = ar.alloc("pA", [128, 1024], F32)
    pB = ar.alloc("pB", [128, 1024], F32)
    pC = ar.alloc("pC", [128, 1024], F32)
    pD = ar.alloc("pD", [128, 1024], F32)
    pmx = ar.alloc("pmx", [128, 64], F32)
    pinv = ar.alloc("pinv", [128, 64], F32)
    phs = ar.alloc("phs", [128, 64], F32)

    pw_units = []
    for it in range(FF // 128):
        pw_units.append((wg_d, wgq_dram, it, 0, 1024))
    for it in range(FF // 128):
        pw_units.append((wu_d, wuq_dram, it, 0, 1024))
    for it in range(D // 128):
        for c0, cw in [(0, 1024), (1024, 1024), (2048, 768)]:
            pw_units.append((wd_d, wdq_dram, it, c0, cw))
    pw_state = {"k": 0, "qdone": {}, "sdone": {}}

    def pw_step(nu=1):
        if upto < 1:
            return
        for _ in range(nu):
            k = pw_state["k"]
            if k >= len(pw_units):
                return
            w_src, w_dst, it, c0, cw = pw_units[k]
            par = k % 2
            if k >= 2:
                wait("y", cV, pw_state["qdone"][k - 2])
            Y(lambda e, w_src=w_src, it=it, par=par, c0=c0, cw=cw:
              e.dma_start(out=pwf[:, par, 0:cw],
                          in_=w_src[it * 128:(it + 1) * 128, c0:c0 + cw]).then_inc(cQS.h, 16))
            cQS.n += 16
            wait("v", cQS)
            if k >= 2:
                wait("v", cQG, pw_state["sdone"][k - 2])
            quant_chain(pwf[:, par, 0:cw], cw, pA, pB, pC, pD, pmx, pinv, phs,
                        pwq[:, par, 0:cw].rearrange("p (a b) -> p a b", b=16))
            pw_state["qdone"][k] = cV.n
            wait("g", cV)
            G(lambda e, w_dst=w_dst, it=it, par=par, c0=c0, cw=cw:
              e.dma_start(out=w_dst[it * 128:(it + 1) * 128, c0:c0 + cw],
                          in_=pwq[:, par, 0:cw]).then_inc(cQG.h, 16))
            cQG.n += 16
            pw_state["sdone"][k] = cQG.n
            pw_state["k"] = k + 1

    def pw_flush():
        pw_step(len(pw_units))

    if upto >= 2:
        ar.mark()
        xtile = ar.alloc("xtile", [128, 2, D], F32)
        phase_barrier()
        trdone, evdone = {}, {}
        for it in range(NT):
            pw_step()
            par = it % 2
            if it >= 2:
                wait("y", cT, trdone[it - 2])
            Y(lambda e, it=it, par=par:
              e.dma_start(out=xtile[:, par, :], in_=x_d[it * 128:(it + 1) * 128, :]).then_inc(cQS.h, 16))
            cQS.n += 16
            wait("t", cQS)
            wait("t", cGC, qg_pads)
            for gq in range(2):
                gi = it * 2 + gq
                bank = gi % 2
                if gi >= 2:
                    wait("t", cS, evdone[gi - 2])
                for q in range(4):
                    kc = gq * 4 + q
                    def tf(e, kc=kc, par=par, bank=bank, inc=(q == 3), q=q):
                        ins = e.transpose(ps[bank][:, q * 128:(q + 1) * 128],
                                          xtile[:, par, kc * 128:(kc + 1) * 128], ident_sb[:])
                        if inc:
                            ins.then_inc(cT.h, 1)
                    Te(tf)
                cT.n += 1
                wait("s", cT)
                S(lambda e, it=it, gq=gq, bank=bank:
                  e.activation(out=xT[:, gq * 4:(gq + 1) * 4, 1 + it * 128: 1 + (it + 1) * 128],
                               in_=ps[bank][:].rearrange("p (a b) -> p a b", b=128),
                               func=AF.Copy).then_inc(cS.h, 1))
                cS.n += 1
                evdone[gi] = cS.n
            trdone[it] = cT.n
        ar.release()
        dump("xT", xT[:], [128, NKC, T + 1], [cS])

    if upto >= 3:
        ar.mark()
        stm = ar.alloc("stm", [128, NT, 4, 64], F32)
        gpt = ar.alloc("gpt", [16, T], F32)

        ar.mark()
        smallw = ar.alloc("smallw", [128, NKC, 4, 64], BF16)
        memgw_sb = ar.alloc("memgw_sb", [128, NKC, 16], BF16)
        phase_barrier()
        for wi, wd_ in enumerate([w1w_d, w2w_d, r1w_d, r2w_d]):
            G(lambda e, wi=wi, wd_=wd_:
              e.dma_start(out=smallw[:, :, wi, :],
                          in_=wd_[:].rearrange("(k p) f -> p k f", p=128)).then_inc(cQG.h, 16))
            cQG.n += 16
        G(lambda e: e.dma_start(out=memgw_sb[:],
                                in_=memgw_d[:].rearrange("(k p) f -> p k f", p=128)).then_inc(cQG.h, 16))
        cQG.n += 16
        ldq = cQG.n
        evq = {}
        for it in (range(1) if "tm_one" in skip else (range(NT) if "tm_smalls" not in skip else [])):
            pw_step()
            bank = it % 2
            wait("t", cQG, ldq)
            wait("t", cQS)
            wait("t", cS)
            if it >= 2:
                wait("t", cS, evq[it - 2])
            for w in range(4):
                shift = 0 if w == 0 else 1
                for kc in range(NKC):
                    last = (w == 3 and kc == NKC - 1)
                    def mf(e, w=w, kc=kc, it=it, bank=bank, shift=shift, last=last):
                        ins = e.matmul(ps[bank][:, w * 64:(w + 1) * 64],
                                       xT[:, kc, shift + it * 128: shift + it * 128 + 128],
                                       smallw[:, kc, w, :],
                                       start=(kc == 0), stop=(kc == NKC - 1),
                                       skip_group_check=True)
                        if last:
                            ins.then_inc(cT.h, 1)
                    Te(mf)
            cT.n += 1
            wait("s", cT)
            S(lambda e, it=it, bank=bank:
              e.activation(out=stm[:, it, :, :],
                           in_=ps[bank][:, 0:256].rearrange("p (a b) -> p a b", b=64),
                           func=AF.Copy).then_inc(cS.h, 1))
            cS.n += 1
            evq[it] = cS.n
        gpq = {}
        for tc in (range(4) if "gpt" not in skip else []):
            bank = 2 + tc % 2
            if tc >= 2:
                wait("t", cS, gpq[tc - 2])
            Te(lambda e, bank=bank:
               e.matmul(ps[bank][0:16, :], memgb_sb[:], ones512[:],
                        start=True, stop=False, skip_group_check=True))
            for kc in range(NKC):
                def gf(e, tc=tc, kc=kc, bank=bank, last=(kc == NKC - 1)):
                    ins = e.matmul(ps[bank][0:16, :], memgw_sb[:, kc, :],
                                   xT[:, kc, 1 + tc * 512: 1 + (tc + 1) * 512],
                                   start=False, stop=(kc == NKC - 1), skip_group_check=True)
                    if last:
                        ins.then_inc(cT.h, 1)
                Te(gf)
            cT.n += 1
            wait("s", cT)
            S(lambda e, tc=tc, bank=bank:
              e.activation(out=gpt[:, tc * 512:(tc + 1) * 512], in_=ps[bank][0:16, :],
                           func=AF.Copy).then_inc(cS.h, 1))
            cS.n += 1
            gpq[tc] = cS.n
        ar.release()
        dump("stm", stm[:], [128, NT, 4, 64], [cS])
        dump("gpt", gpt[:], [16, T], [cS])

        if upto >= 4:
            ar.mark()
            jwt = ar.alloc("jwt", [128, NT, 16, 6], F32)
            rlt = ar.alloc("rlt", [128, NT, 16, 6], F32)
            rl2t = ar.alloc("rl2t", [128, NT, 16, 6], F32)

            ar.mark()
            parts = ar.alloc("parts", [128, NT, 16, 6], F32)
            tmpA = ar.alloc("tmpA", [128, NT, 16], F32)
            tmpB = ar.alloc("tmpB", [128, NT, 16], F32)
            nsqA = ar.alloc("nsqA", [128, NT, 16], F32)
            nsqB = ar.alloc("nsqB", [128, NT, 16], F32)
            nrm = ar.alloc("nrm", [128, NT, 16], F32)
            rinv = ar.alloc("rinv", [128, NT, 16], F32)
            rinvn = ar.alloc("rinvn", [128, NT, 16], F32)

            wait("v", cS)
            wait("v", cT)

            def lines(side):
                a_i, b_i = (0, 1) if side == 0 else (2, 3)
                dst = jwt if side == 0 else rlt
                for p, (a, b) in enumerate(PAIRS4):
                    V(lambda e, a=a, b=b, a_i=a_i, b_i=b_i:
                      e.tensor_tensor(tmpA[:], stm[:, :, a_i, a::4], stm[:, :, b_i, b::4], ALU.mult))
                    V(lambda e, a=a, b=b, a_i=a_i, b_i=b_i:
                      e.tensor_tensor(tmpB[:], stm[:, :, a_i, b::4], stm[:, :, b_i, a::4], ALU.mult))
                    V(lambda e, p=p: e.tensor_tensor(parts[:, :, :, p], tmpA[:], tmpB[:], ALU.subtract))
                V(lambda e: e.tensor_tensor(nsqA[:], parts[:, :, :, 0], parts[:, :, :, 0], ALU.mult))
                for p in range(1, 6):
                    V(lambda e, p=p: e.tensor_tensor(tmpA[:], parts[:, :, :, p], parts[:, :, :, p], ALU.mult))
                    src, dst_n = (nsqA, nsqB) if p % 2 == 1 else (nsqB, nsqA)
                    if p < 5:
                        V(lambda e, src=src, dst_n=dst_n: e.tensor_tensor(dst_n[:], src[:], tmpA[:], ALU.add))
                    else:
                        V(lambda e, src=src, dst_n=dst_n:
                          e.tensor_tensor(dst_n[:], src[:], tmpA[:], ALU.add).then_inc(cV.h, 1))
                cV.n += 1
                wait("s", cV)
                S(lambda e: e.activation(out=nrm[:], in_=nsqB[:], func=AF.Sqrt).then_inc(cS.h, 1))
                cS.n += 1
                wait("v", cS)
                V(lambda e: e.tensor_scalar(nrm[:], nrm[:], 1e-12, None, ALU.max).then_inc(cV.h, 1))
                cV.n += 1
                wait("v", cV)
                V(lambda e: e.reciprocal(rinv[:], nrm[:]).then_inc(cV.h, 1))
                cV.n += 1
                wait("v", cV)
                if side == 0:
                    V(lambda e: e.tensor_scalar(rinvn[:], rinv[:], -1.0, None, ALU.mult))
                    for q in range(6):
                        p, sg = JW_PERM[q], JW_SIGN[q]
                        r = rinv if sg > 0 else rinvn
                        V(lambda e, q=q, p=p, r=r:
                          e.tensor_tensor(dst[:, :, :, q], parts[:, :, :, p], r[:], ALU.mult))
                else:
                    for q in range(6):
                        V(lambda e, q=q: e.tensor_tensor(dst[:, :, :, q], parts[:, :, :, q], rinv[:], ALU.mult))

            lines(0)
            lines(1)
            V(lambda e: e.tensor_scalar(rl2t[:], rlt[:], 2.0, None, ALU.mult))
            ar.release()

            ar.mark()
            ztm = ar.alloc("ztm", [128, 4, 16, 7], F32)
            utm = ar.alloc("utm", [128, 4, 16, 7], F32)
            Zfm = ar.alloc("Zfm", [112, T + 1], F32)
            Uc = ar.alloc("Uc", [112, 512], F32)
            Dc = ar.alloc("Dc", [112, 2, 512], F32)
            dec5 = ar.alloc("dec5", [112, 512], F32)
            UDc = ar.alloc("UDc", [112, 512], F32)

            phase_barrier(("g",))
            wait("v", cT)
            G(lambda e: e.memset(dec5[:], DECAY))
            G(lambda e: e.memset(Zfm[:, 0:1], 0.0).then_inc(cGC.h, 1))
            cGC.n += 1
            wait("v", cGC)

            trd = {}
            tix = 0
            for g in range(3):
                grp = IJ_PAIRS[g * 7:(g + 1) * 7]
                for tc in range(4):
                    for idx, (a, b) in enumerate(grp):
                        V(lambda e, a=a, b=b, idx=idx, tc=tc:
                          e.tensor_tensor(ztm[:, :, :, idx], jwt[:, tc * 4:(tc + 1) * 4, :, a],
                                          jwt[:, tc * 4:(tc + 1) * 4, :, b], ALU.mult))
                        ub = rlt if a == b else rl2t
                        def pu(e, a=a, b=b, idx=idx, tc=tc, ub=ub, last=(idx == len(grp) - 1)):
                            ins = e.tensor_tensor(utm[:, :, :, idx], rlt[:, tc * 4:(tc + 1) * 4, :, a],
                                                  ub[:, tc * 4:(tc + 1) * 4, :, b], ALU.mult)
                            if last:
                                ins.then_inc(cV.h, 1)
                        V(pu)
                    cV.n += 1
                    wait("t", cV)
                    for q in range(4):
                        for which in range(2):
                            src = ztm if which == 0 else utm
                            bank = tix % 2
                            if tix >= 2:
                                wait("t", cV, trd[tix - 2])
                            Te(lambda e, src=src, q=q, bank=bank:
                               e.transpose(ps[bank][0:112, 0:128],
                                           src[:, q, :, :].rearrange("p a b -> p (a b)"),
                                           ident_sb[:]).then_inc(cT.h, 1))
                            cT.n += 1
                            wait("v", cT)
                            if which == 0:
                                V(lambda e, q=q, tc=tc, bank=bank:
                                  e.tensor_copy(Zfm[:, 1 + tc * 512 + q * 128: 1 + tc * 512 + (q + 1) * 128],
                                                ps[bank][0:112, 0:128]).then_inc(cV.h, 1))
                            else:
                                V(lambda e, q=q, bank=bank:
                                  e.tensor_copy(Uc[:, q * 128:(q + 1) * 128],
                                                ps[bank][0:112, 0:128]).then_inc(cV.h, 1))
                            cV.n += 1
                            trd[tix] = cV.n
                            tix += 1
                    par = tc % 2
                    if tc == 0:
                        V(lambda e, par=par:
                          e.tensor_tensor_scan(Dc[:, par, :], dec5[:], Zfm[:, 0:512],
                                               0.0, ALU.mult, ALU.add))
                    else:
                        V(lambda e, tc=tc, par=par:
                          e.tensor_tensor_scan(Dc[:, par, :], dec5[:], Zfm[:, tc * 512:(tc + 1) * 512],
                                               Dc[:, 1 - par, 511:512], ALU.mult, ALU.add))
                    V(lambda e, par=par:
                      e.tensor_tensor(UDc[:], Uc[:], Dc[:, par, :], ALU.mult).then_inc(cV.h, 1))
                    cV.n += 1
                    wait("t", cV)
                    Te(lambda e, g=g, tc=tc:
                       e.matmul(ps[4 + tc][0:16, :], esel_sb[:], UDc[:],
                                start=(g == 0), stop=(g == 2),
                                skip_group_check=True).then_inc(cT.h, 1))
                    cT.n += 1
                    wait("v", cT)
            ar.release()

            ar.mark()
            sigms = ar.alloc("sigms", [16, T], F32)
            gm = ar.alloc("gm", [16, T], F32)
            gm2 = ar.alloc("gm2", [16, T], F32)
            wait("s", cT)
            wait("s", cV)
            for ci in range(4):
                S(lambda e, ci=ci:
                  e.activation(out=sigms[:, ci * 512:(ci + 1) * 512], in_=ps[4 + ci][0:16, :],
                               func=AF.Sigmoid, scale=mscale_sb[:]))
            S(lambda e: e.activation(out=gm[:], in_=gpt[:], func=AF.Sigmoid).then_inc(cS.h, 1))
            cS.n += 1
            wait("v", cS)
            V(lambda e: e.tensor_tensor(gm2[:], sigms[:], gm[:], ALU.mult).then_inc(cV.h, 1))
            cV.n += 1
            wait("t", cV)
            for ci in range(4):
                def mm2(e, ci=ci, last=(ci == 3)):
                    ins = e.matmul(ps[ci][0:1, :], sixt[:], gm2[:, ci * 512:(ci + 1) * 512],
                                   start=True, stop=True, skip_group_check=True)
                    if last:
                        ins.then_inc(cT.h, 1)
                Te(mm2)
            cT.n += 1
            wait("s", cT)
            for ci in range(4):
                def ge_(e, ci=ci, last=(ci == 3)):
                    ins = e.activation(out=grow[0:1, ci * 512:(ci + 1) * 512], in_=ps[ci][0:1, :],
                                       func=AF.Copy)
                    if last:
                        ins.then_inc(cS.h, 1)
                S(ge_)
            cS.n += 1
            dump("msc", sigms[:], [16, T], [cS])
            ar.release()
            ar.release()
            dump("grow", grow[:], [1, T], [cS])
        ar.release()

    if upto >= 5:
        ar.mark()
        KT = ar.alloc("KT", [128, NKC, T], BF16)
        vhat = ar.alloc("vhat", [128, NT, H, 65], BF16)
        qkvb_sb = ar.alloc("qkvb_sb", [1, 3 * D], F32)
        wq_res = ar.alloc("wq_res", [128, NKC, D], BF16)
        phase_barrier()
        Y(lambda e: e.dma_start(out=qkvb_sb[:], in_=qkvb_d).then_inc(cQS.h, 16))
        cQS.n += 16
        G(lambda e: e.memset(vhat[:, :, :, 64:65], 1.0).then_inc(cGC.h, 1))
        cGC.n += 1
        qg_vpad = cGC.n
        for kc in range(NKC):
            G(lambda e, kc=kc:
              e.dma_start(out=wq_res[:, kc, :],
                          in_=qkvw_d[kc * 128:(kc + 1) * 128, 0:1024]).then_inc(cQG.h, 16))
            cQG.n += 16

        ar.mark()
        wbuf = ar.alloc("wbuf", [128, NKC, 1024], BF16)
        phase_barrier(("g",))
        for kc in range(NKC):
            G(lambda e, kc=kc:
              e.dma_start(out=wbuf[:, kc, :],
                          in_=qkvw_d[kc * 128:(kc + 1) * 128, 1024:2048]).then_inc(cQG.h, 16))
            cQG.n += 16
        ldk = cQG.n
        for tc in range(4):
            for fc in range(NKC):
                pw_step()
                bank = (tc * NKC + fc) % 2
                wait("t", cQG, ldk)
                wait("t", cQS)
                wait("t", cS)
                Te(lambda e, fc=fc, bank=bank:
                   e.matmul(ps[bank][:, :], qkvb_sb[0:1, 1024 + fc * 128: 1024 + (fc + 1) * 128],
                            ones512[:], start=True, stop=False, skip_group_check=True))
                for kc in range(NKC):
                    def kf(e, kc=kc, tc=tc, fc=fc, bank=bank, last=(kc == NKC - 1)):
                        ins = e.matmul(ps[bank][:, :], wbuf[:, kc, fc * 128:(fc + 1) * 128],
                                       xT[:, kc, 1 + tc * 512: 1 + (tc + 1) * 512],
                                       start=False, stop=(kc == NKC - 1), skip_group_check=True)
                        if last:
                            ins.then_inc(cT.h, 1)
                    Te(kf)
                cT.n += 1
                wait("s", cT)
                S(lambda e, fc=fc, tc=tc, bank=bank:
                  e.activation(out=KT[:, fc, tc * 512:(tc + 1) * 512], in_=ps[bank][:, :],
                               func=AF.Copy).then_inc(cS.h, 1))
                cS.n += 1
        kt_mm_done = cT.n
        for kc in range(NKC):
            wait("g", cT, kt_mm_done)
            G(lambda e, kc=kc:
              e.dma_start(out=wbuf[:, kc, :],
                          in_=qkvw_d[kc * 128:(kc + 1) * 128, 2048:3072]).then_inc(cQG.h, 16))
            cQG.n += 16
        ldv = cQG.n
        for it in range(NT):
            for vc in range(2):
                pw_step()
                bank = 2 + (it * 2 + vc) % 2
                wait("t", cQG, ldv)
                wait("t", cGC, qg_vpad)
                wait("t", cS)
                Te(lambda e, vc=vc, bank=bank:
                   e.matmul(ps[bank][:, :], ones1[:], qkvb_sb[0:1, 2048 + vc * 512: 2048 + (vc + 1) * 512],
                            start=True, stop=False, skip_group_check=True))
                for kc in range(NKC):
                    def vf(e, kc=kc, it=it, vc=vc, bank=bank, last=(kc == NKC - 1)):
                        ins = e.matmul(ps[bank][:, :], xT[:, kc, 1 + it * 128: 1 + (it + 1) * 128],
                                       wbuf[:, kc, vc * 512:(vc + 1) * 512],
                                       start=False, stop=(kc == NKC - 1), skip_group_check=True)
                        if last:
                            ins.then_inc(cT.h, 1)
                    Te(vf)
                cT.n += 1
                wait("s", cT)
                S(lambda e, it=it, vc=vc, bank=bank:
                  e.activation(out=vhat[:, it, vc * 8:(vc + 1) * 8, 0:64],
                               in_=ps[bank][:, :].rearrange("p (a b) -> p a b", b=64),
                               func=AF.Copy).then_inc(cS.h, 1))
                cS.n += 1
        ar.release()
        snap("snap_p1b", vhat[:], [128, NT, H, 65])
        dump("kt", KT[:], [128, NKC, T], [cS])
        dump("vhat", vhat[:], [128, NT, H, 65], [cS])

        if upto >= 6:
            ar.mark()
            Qc = ar.alloc("Qc", [128, NKC, 512], BF16)
            Pt = ar.alloc("Pt", [128, 2, 512], BF16)
            Ptmp = ar.alloc("Ptmp", [128, 128], BF16)
            rdrow = ar.alloc("rdrow", [128, 1024], F32)
            onesrow = ar.alloc("onesrow", [128, 64], F32)
            sqb = ar.alloc("sqb", [128, 1024], F32)
            y1 = ar.alloc("y1", [128, 1024], BF16)

            phase_barrier(("g",))
            G(lambda e: e.memset(onesrow[:], 1.0).then_inc(cGC.h, 1))
            cGC.n += 1
            wait("t", cGC)
            sq_done = {}
            asm_done = {}
            ynst = {}
            for c in range(4):
                for fc in range(NKC):
                    bank = fc % 2
                    wait("t", cS)
                    wait("t", cQG)
                    Te(lambda e, fc=fc, bank=bank:
                       e.matmul(ps[bank][:, :], qkvb_sb[0:1, fc * 128:(fc + 1) * 128], ones512[:],
                                start=True, stop=False, skip_group_check=True))
                    for kc in range(NKC):
                        def qf(e, kc=kc, fc=fc, c=c, bank=bank, last=(kc == NKC - 1)):
                            ins = e.matmul(ps[bank][:, :], wq_res[:, kc, fc * 128:(fc + 1) * 128],
                                           xT[:, kc, 1 + c * 512: 1 + (c + 1) * 512],
                                           start=False, stop=(kc == NKC - 1), skip_group_check=True)
                            if last:
                                ins.then_inc(cT.h, 1)
                        Te(qf)
                    cT.n += 1
                    wait("s", cT)
                    S(lambda e, fc=fc, bank=bank:
                      e.activation(out=Qc[:, fc, :], in_=ps[bank][:, :], func=AF.Copy).then_inc(cS.h, 1))
                    cS.n += 1
                qc_done = cS.n
                if c == 0:
                    snap("snap_qc", vhat[:], [128, NT, H, 65])
                if "vhatq" in dbg and c == 0:
                    d_vq = nc.dram_tensor("dbg_vhatq", [128, NT, H, 65], F32, kind="ExternalOutput").ap()
                    dbg_names.append("vhatq")
                    wait("g", cS, qc_done)
                    G(lambda e, d=d_vq: e.dma_start(out=d, in_=vhat[:]).then_inc(cQG.h, 16))
                    cQG.n += 16
                if "dbgqc0" in dbg and c == 0:
                    d_qc0 = nc.dram_tensor("dbg_qc0", [128, NKC, 512], F32, kind="ExternalOutput").ap()
                    dbg_names.append("qc0")
                    wait("g", cS, qc_done)
                    G(lambda e, d_qc0=d_qc0: e.dma_start(out=d_qc0, in_=Qc[:]).then_inc(cQG.h, 16))
                    cQG.n += 16

                hplist = [1] if "only_hp1" in skip else range(8)
                for hp in hplist:
                    pw_step()
                    if (c, hp) != (0, 0) and "only_hp1" not in skip:
                        prev = (c, hp - 1) if hp > 0 else (c - 1, 7)
                        wait("t", cS, sq_done[prev])
                        wait("t", cV, asm_done[prev])
                    last_mask = 0
                    for sub in range(2):
                        h = 2 * hp + sub
                        obank = 2 + sub
                        kchunk = h // 2
                        krow = (h % 2) * 64
                        if "mirror0" in skip and hp == 1:
                            kchunk = 0
                            h = sub
                        if "mirrorqk" in skip and hp == 1:
                            kchunk = 0
                        if "mirrorv" in skip and hp == 1:
                            h = sub
                        expd, pvd = {}, {}
                        njt = 4 * c + 4
                        if "hp1_j1" in skip and hp == 1:
                            njt = 1
                        for j in range(njt):
                            par = j % 2
                            off = max(0, (j - 4 * c) * 128)
                            wait("t", cS, qc_done)
                            if j >= 2:
                                wait("t", cS, expd[j - 2])
                            Te(lambda e, j=j, par=par, off=off, kchunk=kchunk, krow=krow:
                               e.matmul(ps[par][:, off:512],
                                        KT[krow:krow + 64, kchunk, j * 128:(j + 1) * 128],
                                        Qc[krow:krow + 64, kchunk, off:512],
                                        start=True, stop=True,
                                        skip_group_check=True).then_inc(cT.h, 1))
                            cT.n += 1
                            wait("s", cT)
                            if "dbgst" in dbg and (c, hp, sub, j) == (0, 1, 0, 1):
                                d_st = nc.dram_tensor("dbg_st", [128, 512], F32, kind="ExternalOutput").ap()
                                dbg_names.append("st")
                                stb = ar.alloc("stb", [128, 512], F32)
                                S(lambda e: e.copy(stb[:, 128:512], ps[1][:, 128:512]).then_inc(cS.h, 1))
                                cS.n += 1
                                wait("g", cS)
                                G(lambda e, d_st=d_st: e.dma_start(out=d_st, in_=stb[:]).then_inc(cQG.h, 16))
                                cQG.n += 16
                            if j >= 2:
                                wait("s", cT, pvd[j - 2])
                            diag = j >= 4 * c
                            if diag:
                                if last_mask:
                                    wait("s", cV, last_mask)
                                if off + 128 < 512:
                                    S(lambda e, par=par, off=off:
                                      e.activation(out=Ptmp[:], in_=ps[par][:, off:off + 128],
                                                   func=AF.Exp, scale=0.125))
                                    S(lambda e, par=par, off=off:
                                      e.activation(out=Pt[:, par, off + 128:512],
                                                   in_=ps[par][:, off + 128:512],
                                                   func=AF.Exp, scale=0.125).then_inc(cS.h, 1))
                                else:
                                    S(lambda e, par=par, off=off:
                                      e.activation(out=Ptmp[:], in_=ps[par][:, off:off + 128],
                                                   func=AF.Exp, scale=0.125).then_inc(cS.h, 1))
                                cS.n += 1
                                expd[j] = cS.n
                                wait("v", cS)
                                V(lambda e, par=par, off=off:
                                  e.tensor_tensor(Pt[:, par, off:off + 128], Ptmp[:], tri_sb[:],
                                                  ALU.mult).then_inc(cV.h, 1))
                                cV.n += 1
                                last_mask = cV.n
                                if "dbgpt" in dbg and (c, hp, sub, j) == (0, 1, 0, 0):
                                    d_pt = nc.dram_tensor("dbg_pt", [128, 512], F32, kind="ExternalOutput").ap()
                                    dbg_names.append("pt")
                                    wait("g", cV)
                                    G(lambda e, d_pt=d_pt, par=par: e.dma_start(out=d_pt, in_=Pt[:, par, :]).then_inc(cQG.h, 16))
                                    cQG.n += 16
                                wait("t", cV, last_mask)
                            else:
                                S(lambda e, par=par:
                                  e.activation(out=Pt[:, par, :], in_=ps[par][:, :],
                                               func=AF.Exp, scale=0.125).then_inc(cS.h, 1))
                                cS.n += 1
                                expd[j] = cS.n
                                wait("t", cS, expd[j])
                            Te(lambda e, j=j, h=h, par=par, off=off, obank=obank, njt=njt:
                               e.matmul(ps[obank][0:65, off:512], vhat[:, j, h, :],
                                        Pt[:, par, off:512],
                                        start=(j == 0), stop=(j == njt - 1),
                                        skip_group_check=True).then_inc(cT.h, 1))
                            cT.n += 1
                            pvd[j] = cT.n
                    wait("s", cT)
                    if "dbgov" in dbg and (c, hp) == (0, 1):
                        d_ov = nc.dram_tensor("dbg_ov", [128, 512], F32, kind="ExternalOutput").ap()
                        dbg_names.append("ov")
                        ovb = ar.alloc("ovb", [128, 512], F32)
                        S(lambda e: e.copy(ovb[0:64, :], ps[2][0:64, :]))
                        S(lambda e: e.copy(ovb[64:65, :], ps[2][64:65, :]).then_inc(cS.h, 1))
                        cS.n += 1
                        wait("g", cS)
                        G(lambda e, d_ov=d_ov: e.dma_start(out=d_ov, in_=ovb[:]).then_inc(cQG.h, 16))
                        cQG.n += 16
                    wait("v", cT)
                    if (c, hp) != (0, 0) and "only_hp1" not in skip:
                        prev = (c, hp - 1) if hp > 0 else (c - 1, 7)
                        wait("s", cV, asm_done[prev])
                        wait("v", cV, asm_done[prev])
                    V(lambda e: e.reciprocal(rdrow[64:65, 0:512], ps[2][64:65, :]))
                    V(lambda e: e.reciprocal(rdrow[64:65, 512:1024], ps[3][64:65, :]).then_inc(cV.h, 1))
                    cV.n += 1
                    S(lambda e: e.copy(sqb[0:64, 0:512], ps[2][0:64, :]))
                    S(lambda e: e.copy(sqb[0:64, 512:1024], ps[3][0:64, :]).then_inc(cS.h, 1))
                    cS.n += 1
                    sq_done[(c, hp)] = cS.n
                    wait("t", cV)
                    Te(lambda e: e.matmul(ps[4][0:64, :], onesrow[64:65, :], rdrow[64:65, 0:512],
                                          start=True, stop=True, skip_group_check=True))
                    Te(lambda e: e.matmul(ps[5][0:64, :], onesrow[64:65, :], rdrow[64:65, 512:1024],
                                          start=True, stop=True,
                                          skip_group_check=True).then_inc(cT.h, 1))
                    cT.n += 1
                    wait("v", cT)
                    wait("v", cS)
                    if (c, hp) in ynst:
                        wait("v", cQS, ynst[(c, hp)])
                    V(lambda e: e.tensor_tensor(y1[0:64, 0:512], sqb[0:64, 0:512], ps[4][0:64, :], ALU.mult))
                    V(lambda e: e.tensor_tensor(y1[0:64, 512:1024], sqb[0:64, 512:1024],
                                                ps[5][0:64, :], ALU.mult).then_inc(cV.h, 1))
                    cV.n += 1
                    asm_done[(c, hp)] = cV.n
                    if False and "dbgsq1" in dbg and (c, hp) == (0, 1):
                        d_sq1 = nc.dram_tensor("dbg_sq1", [128, 1024], F32, kind="ExternalOutput").ap()
                        d_rd1 = nc.dram_tensor("dbg_rd1", [128, 1024], F32, kind="ExternalOutput").ap()
                        dbg_names.extend(["sq1", "rd1"])
                        wait("g", cS)
                        wait("g", cV)
                        G(lambda e, d=d_sq1: e.dma_start(out=d, in_=sqb[:]).then_inc(cQG.h, 16))
                        cQG.n += 16
                        G(lambda e, d=d_rd1: e.dma_start(out=d, in_=rdrow[:]).then_inc(cQG.h, 16))
                        cQG.n += 16
                    wait("y", cV)
                    Y(lambda e, hp=hp, c=c:
                      e.dma_start(out=yn_dram[hp * 128: hp * 128 + 64, c * 512:(c + 1) * 512],
                                  in_=y1[0:64, 0:512]).then_inc(cQS.h, 16))
                    cQS.n += 16
                    Y(lambda e, hp=hp, c=c:
                      e.dma_start(out=yn_dram[hp * 128 + 64: hp * 128 + 128, c * 512:(c + 1) * 512],
                                  in_=y1[0:64, 512:1024]).then_inc(cQS.h, 16))
                    cQS.n += 16
                    nxt = (c, hp + 1) if hp < 7 else (c + 1, 0)
                    ynst[nxt] = cQS.n
                    if (c, hp) == (0, 0):
                        snap("snap_h0", vhat[:], [128, NT, H, 65])
                    if (c, hp) == (0, 1):
                        snap("snap_h1", vhat[:], [128, NT, H, 65])
                    if False and "vhath0" in dbg and (c, hp) == (0, 0):
                        d_vh0 = nc.dram_tensor("dbg_vhath0", [128, NT, H, 65], F32, kind="ExternalOutput").ap()
                        dbg_names.append("vhath0")
                        wait("g", cV)
                        wait("g", cS)
                        G(lambda e, d=d_vh0: e.dma_start(out=d, in_=vhat[:]).then_inc(cQG.h, 16))
                        cQG.n += 16
            dump("qc", Qc[:], [128, NKC, 512], [cS, cT])
            dump("vhat2", vhat[:], [128, NT, H, 65], [cS, cT, cV])
            dump("onesrow", onesrow[:], [128, 64], [cV, cT])
            dump("rdrow", rdrow[:], [128, 1024], [cV, cT])
            dump("sqb", sqb[:], [128, 1024], [cS])
            ar.release()
            if "yn" in dbg:
                d = nc.dram_tensor("dbg_yn", [D, T], F32, kind="ExternalOutput").ap()
                dbg_names.append("yn")
                wait("g", cQG)
                wait("g", cQS)
                G(lambda e, d=d: e.dma_start(out=d, in_=yn_dram[:, :]).then_inc(cQG.h, 16))
                cQG.n += 16
        ar.release()

    if upto >= 7:
        ar.mark()
        memvw_sb = ar.alloc("memvw_sb", [128, NKC, D], BF16)
        outw_sb = ar.alloc("outw_sb", [128, NKC, D], BF16)
        ync = ar.alloc("ync", [128, NKC, 512], BF16)
        yfin = ar.alloc("yfin", [128, NKC, 512], BF16)
        gbsb = ar.alloc("gbsb", [128, 512], F32)
        y2 = ar.alloc("y2", [128, 512], F32)
        xrl = ar.alloc("xrl", [128, 2, D], F32)
        rbuf = ar.alloc("rbuf", [128, D], F32)
        x2buf = ar.alloc("x2buf", [128, D], F32)
        bst = ar.alloc("bst", [128, 12], F32)
        mv2 = ar.alloc("mv2", [128, 2], F32)
        veps = ar.alloc("veps", [128, 1], F32)
        sdv = ar.alloc("sdv", [128, 1], F32)
        rstd = ar.alloc("rstd", [128, 1], F32)
        mup = ar.alloc("mup", [128, 1], F32)

        phase_barrier()
        G(lambda e: e.dma_start(out=memvw_sb[:],
                                in_=memvw_d[:].rearrange("(k p) f -> p k f", p=128)).then_inc(cQG.h, 16))
        cQG.n += 16
        G(lambda e: e.dma_start(out=outw_sb[:],
                                in_=outw_d[:].rearrange("(k p) f -> p k f", p=128)).then_inc(cQG.h, 16))
        cQG.n += 16
        ld4 = cQG.n
        p4v = {}
        for c in range(4):
            wait("y", cV)
            Y(lambda e, c=c:
              e.dma_start(out=ync[:], in_=yn_dram[:, c * 512:(c + 1) * 512]
                          .rearrange("(k p) t -> p k t", p=128)).then_inc(cQS.h, 16))
            cQS.n += 16
            wait("t", cS)
            wait("t", cQG, ld4)
            Te(lambda e, c=c: e.matmul(ps[5][:, :], ones1[:], grow[0:1, c * 512:(c + 1) * 512],
                                       start=True, stop=True,
                                       skip_group_check=True).then_inc(cT.h, 1))
            cT.n += 1
            wait("s", cT)
            S(lambda e: e.activation(out=gbsb[:], in_=ps[5][:, :], func=AF.Copy).then_inc(cS.h, 1))
            cS.n += 1
            ldyn = cQS.n
            for hp in range(NKC):
                if p4v:
                    wait("t", cV, max(p4v.values()))
                Te(lambda e, hp=hp:
                   e.matmul(ps[6][:, :], memvb_sb[0:1, hp * 128:(hp + 1) * 128], ones512[:],
                            start=True, stop=False, skip_group_check=True))
                for kc in range(NKC):
                    def mvf(e, kc=kc, hp=hp, c=c, last=(kc == NKC - 1)):
                        ins = e.matmul(ps[6][:, :], memvw_sb[:, kc, hp * 128:(hp + 1) * 128],
                                       xT[:, kc, 1 + c * 512: 1 + (c + 1) * 512],
                                       start=False, stop=(kc == NKC - 1), skip_group_check=True)
                        if last:
                            ins.then_inc(cT.h, 1)
                    Te(mvf)
                cT.n += 1
                wait("v", cT)
                wait("v", cS)
                wait("v", cQS, ldyn)
                V(lambda e: e.tensor_tensor(y2[:], gbsb[:], ps[6][:, :], ALU.mult))
                V(lambda e, hp=hp:
                  e.tensor_tensor(yfin[:, hp, :], ync[:, hp, :], y2[:], ALU.add).then_inc(cV.h, 1))
                cV.n += 1
                p4v[hp] = cV.n
            for ti in range(4):
                it = c * 4 + ti
                wait("y", cV)
                Y(lambda e, it=it, ti=ti:
                  e.dma_start(out=xrl[:, ti % 2, :], in_=x_d[it * 128:(it + 1) * 128, :]).then_inc(cQS.h, 16))
                cQS.n += 16
                ldx = cQS.n
                wait("t", cV, max(p4v.values()))
                wait("t", cV)
                for half in range(2):
                    bank = half
                    Te(lambda e, half=half, bank=bank:
                       e.matmul(ps[bank][:, :], ones1[:], outb_sb[0:1, half * 512:(half + 1) * 512],
                                start=True, stop=False, skip_group_check=True))
                    for kc in range(NKC):
                        def of(e, kc=kc, ti=ti, half=half, bank=bank,
                               last=(half == 1 and kc == NKC - 1)):
                            ins = e.matmul(ps[bank][:, :], yfin[:, kc, ti * 128:(ti + 1) * 128],
                                           outw_sb[:, kc, half * 512:(half + 1) * 512],
                                           start=False, stop=(kc == NKC - 1), skip_group_check=True)
                            if last:
                                ins.then_inc(cT.h, 1)
                        Te(of)
                cT.n += 1
                wait("v", cT)
                wait("v", cQS, ldx)
                wait("v", cS)
                wait("v", cQG)
                V(lambda e, ti=ti: e.tensor_tensor(rbuf[:, 0:512], xrl[:, ti % 2, 0:512], ps[0][:, :], ALU.add))
                V(lambda e, ti=ti: e.tensor_tensor(rbuf[:, 512:1024], xrl[:, ti % 2, 512:1024], ps[1][:, :], ALU.add).then_inc(cV.h, 1))
                cV.n += 1
                ln_tile(rbuf[:], x2buf[:], bst, mv2, veps, sdv, rstd, mup)
                wait("g", cS)
                G(lambda e, it=it:
                  e.dma_start(out=x2_dram[it * 128:(it + 1) * 128, :], in_=x2buf[:]).then_inc(cQG.h, 16))
                cQG.n += 16
        ar.release()
        if "x2" in dbg:
            d_x2 = nc.dram_tensor("dbg_x2", [T, D], F32, kind="ExternalOutput").ap()
            dbg_names.append("x2")
            wait("g", cQG)
            wait("g", cS)
            G(lambda e, d=d_x2: e.dma_start(out=d, in_=x2_dram[:, :]).then_inc(cQG.h, 16))
            cQG.n += 16

    if upto >= 8:
        pw_flush()
        ar.release()
        ar.mark()
        xq_in = ar.alloc("xq_in", [128, 2, D], F32)
        xq_out = ar.alloc("xq_out", [128, 2, D], BF16)
        fA = ar.alloc("fA", [128, D], F32)
        fB = ar.alloc("fB", [128, D], F32)
        fC = ar.alloc("fC", [128, D], F32)
        fD = ar.alloc("fD", [128, D], F32)
        fmx = ar.alloc("fmx", [128, 64], F32)
        finv = ar.alloc("finv", [128, 64], F32)
        fhs = ar.alloc("fhs", [128, 64], F32)
        phase_barrier()
        xq_q, xq_s = {}, {}
        for it in range(NT):
            par = it % 2
            if it >= 2:
                wait("y", cV, xq_q[it - 2])
            Y(lambda e, it=it, par=par:
              e.dma_start(out=xq_in[:, par, :], in_=x2_dram[it * 128:(it + 1) * 128, :]).then_inc(cQS.h, 16))
            cQS.n += 16
            wait("v", cQS)
            if it >= 2:
                wait("v", cQC, xq_s[it - 2])
            quant_chain(xq_in[:, par, :], D, fA, fB, fC, fD, fmx, finv, fhs,
                        xq_out[:, par, :].rearrange("p (a b) -> p a b", b=16))
            xq_q[it] = cV.n
            wait("s", cV)
            S(lambda e, it=it, par=par:
              e.dma_start(out=x2q_dram[it * 128:(it + 1) * 128, :], in_=xq_out[:, par, :]).then_inc(cQC.h, 16))
            cQC.n += 16
            xq_s[it] = cQC.n
        ar.release()

        ar.mark()
        hq_tm = ar.alloc("hq_tm", [128, NT, FF], BF16)
        ar.mark()
        x2qT = ar.alloc("x2qT", [128, NKC, T], BF16)
        wslc = ar.alloc("wslc", [128, 2, NKC, 512], BF16)
        gsil = ar.alloc("gsil", [128, 512], F32)
        hbuf = ar.alloc("hbuf", [128, 512], F32)
        gA = ar.alloc("gA", [128, 512], F32)
        gB = ar.alloc("gB", [128, 512], F32)
        gC = ar.alloc("gC", [128, 512], F32)
        gD = ar.alloc("gD", [128, 512], F32)
        gmx = ar.alloc("gmx", [128, 32], F32)
        ginv = ar.alloc("ginv", [128, 32], F32)
        ghs = ar.alloc("ghs", [128, 32], F32)

        phase_barrier()
        wait("y", cQC)
        for kc in range(NKC):
            Y(lambda e, kc=kc:
              e.dma_start(out=x2qT[:, kc, :], in_=x2q_dram[:, kc * 128:(kc + 1) * 128],
                          transpose=True).then_inc(cQS.h, 16))
            cQS.n += 16
        ldxq = cQS.n
        for ici, (ic0, icw) in enumerate(IC_CHUNKS):
            wait("y", cV)
            wait("y", cQG)
            for wi, wsrc in enumerate([wgq_dram, wuq_dram]):
                for kc in range(NKC):
                    Y(lambda e, wi=wi, wsrc=wsrc, kc=kc, ic0=ic0, icw=icw:
                      e.dma_start(out=wslc[:, wi, kc, 0:icw],
                                  in_=wsrc[ic0:ic0 + icw, kc * 128:(kc + 1) * 128],
                                  transpose=True).then_inc(cQS.h, 16))
                    cQS.n += 16
            ldw = cQS.n
            for it in range(NT):
                gidx5 = ici * NT + it
                gb_ = (it % 2) * 2
                ub_ = gb_ + 1
                wait("t", cQS, max(ldw, ldxq))
                if gidx5 >= 2:
                    wait("t", cV, hq_done[gidx5 - 2])
                for wi, bank in [(0, gb_), (1, ub_)]:
                    for kc in range(NKC):
                        def guf(e, wi=wi, kc=kc, it=it, icw=icw, bank=bank,
                                last=(wi == 1 and kc == NKC - 1)):
                            ins = e.matmul(ps[bank][:, 0:icw], x2qT[:, kc, it * 128:(it + 1) * 128],
                                           wslc[:, wi, kc, 0:icw],
                                           start=(kc == 0), stop=(kc == NKC - 1), skip_group_check=True)
                            if last:
                                ins.then_inc(cT.h, 1)
                        Te(guf)
                cT.n += 1
                wait("s", cT)
                wait("s", cV)
                S(lambda e, gb_=gb_, icw=icw:
                  e.activation(out=gsil[:, 0:icw], in_=ps[gb_][:, 0:icw], func=AF.Silu).then_inc(cS.h, 1))
                cS.n += 1
                wait("v", cS)
                V(lambda e, ub_=ub_, icw=icw:
                  e.tensor_tensor(hbuf[:, 0:icw], gsil[:, 0:icw], ps[ub_][:, 0:icw], ALU.mult).then_inc(cV.h, 1))
                cV.n += 1
                quant_chain(hbuf[:, 0:icw], icw, gA, gB, gC, gD, gmx, ginv, ghs,
                            hq_tm[:, it, ic0:ic0 + icw].rearrange("p (a b) -> p a b", b=16))
                hq_done = hq_done if gidx5 > 0 else {}
                hq_done[gidx5] = cV.n
        ar.release()

        wait("g", cV)
        for it in range(NT):
            G(lambda e, it=it:
              e.dma_start(out=hq_dram[it * 128:(it + 1) * 128, :], in_=hq_tm[:, it, :]).then_inc(cQG.h, 16))
            cQG.n += 16
        hq_st = cQG.n
        ar.release()

        ar.mark()
        wdqT = ar.alloc("wdqT", [128, 22, D], BF16)
        hqT = ar.alloc("hqT", [128, 2, 512], BF16)
        xr2 = ar.alloc("xr2", [128, 2, D], F32)
        rb2 = ar.alloc("rb2", [128, 2, D], F32)
        ob2 = ar.alloc("ob2", [128, 2, D], F32)
        bst2 = ar.alloc("bst2", [128, 12], F32)
        mv22 = ar.alloc("mv22", [128, 2], F32)
        veps2 = ar.alloc("veps2", [128, 1], F32)
        sdv2 = ar.alloc("sdv2", [128, 1], F32)
        rstd2 = ar.alloc("rstd2", [128, 1], F32)
        mup2 = ar.alloc("mup2", [128, 1], F32)

        phase_barrier()
        for ic in range(22):
            Y(lambda e, ic=ic:
              e.dma_start(out=wdqT[:, ic, :], in_=wdq_dram[:, ic * 128:(ic + 1) * 128],
                          transpose=True).then_inc(cQS.h, 16))
            cQS.n += 16
        ldwd = cQS.n
        for tg in range(4):
            hq_ld = {}
            for ic in range(22):
                gidx6 = tg * 22 + ic
                par = ic % 2
                if gidx6 >= 2:
                    wait("y", cT, hq_mm[gidx6 - 2])
                Y(lambda e, tg=tg, ic=ic, par=par:
                  e.dma_start(out=hqT[:, par, :], in_=hq_dram[tg * 512:(tg + 1) * 512, ic * 128:(ic + 1) * 128],
                              transpose=True).then_inc(cQS.h, 16))
                cQS.n += 16
                hq_ld[ic] = cQS.n
                wait("t", cQS, max(hq_ld[ic], ldwd))
                wait("t", cQG, hq_st)
                if ic == 0:
                    wait("t", cS)
                    wait("t", cV)
                for ti in range(4):
                    for half in range(2):
                        bank = ti * 2 + half
                        def df(e, ic=ic, ti=ti, half=half, par=par, bank=bank,
                               last=(ti == 3 and half == 1)):
                            ins = e.matmul(ps[bank][:, :], hqT[:, par, ti * 128:(ti + 1) * 128],
                                           wdqT[:, ic, half * 512:(half + 1) * 512],
                                           start=(ic == 0), stop=(ic == 21), skip_group_check=True)
                            if last:
                                ins.then_inc(cT.h, 1)
                        Te(df)
                cT.n += 1
                hq_mm = hq_mm if gidx6 > 0 else {}
                hq_mm[gidx6] = cT.n
            dn_done = cT.n
            for ti in range(4):
                it = tg * 4 + ti
                par = ti % 2
                if it >= 2:
                    wait("y", cV, add_done[it - 2])
                Y(lambda e, it=it, par=par:
                  e.dma_start(out=xr2[:, par, :], in_=x2_dram[it * 128:(it + 1) * 128, :]).then_inc(cQS.h, 16))
                cQS.n += 16
                ldx2 = cQS.n
                wait("v", cT, dn_done)
                wait("v", cQS, ldx2)
                wait("v", cS)
                wait("v", cQG)
                V(lambda e, ti=ti, par=par:
                  e.tensor_tensor(rb2[:, par, 0:512], xr2[:, par, 0:512], ps[ti * 2][:, :], ALU.add))
                V(lambda e, ti=ti, par=par:
                  e.tensor_tensor(rb2[:, par, 512:1024], xr2[:, par, 512:1024], ps[ti * 2 + 1][:, :], ALU.add).then_inc(cV.h, 1))
                cV.n += 1
                add_done = add_done if it > 0 else {}
                add_done[it] = cV.n
                ln_tile(rb2[:, par, :], ob2[:, par, :], bst2, mv22, veps2, sdv2, rstd2, mup2)
                wait("g", cS)
                G(lambda e, it=it, par=par:
                  e.dma_start(out=out_d[it * 128:(it + 1) * 128, :], in_=ob2[:, par, :]).then_inc(cQG.h, 16))
                cQG.n += 16
        ar.release()

    if upto >= 8:
        for nm, dr, shp in (("wgq", wgq_dram, [FF, D]), ("wuq", wuq_dram, [FF, D]),
                            ("wdq", wdq_dram, [D, FF]), ("x2f", x2_dram, [T, D]),
                            ("x2q", x2q_dram, [T, D]), ("hqd", hq_dram, [T, FF])):
            if nm in dbg:
                dd = nc.dram_tensor("dbg_" + nm, shp, F32, kind="ExternalOutput").ap()
                dbg_names.append(nm)
                wait("g", cQG)
                wait("g", cQC)
                wait("g", cS)
                wait("g", cV)
                G(lambda e, dd=dd, dr=dr: e.dma_start(out=dd, in_=dr[:, :]).then_inc(cQG.h, 16))
                cQG.n += 16

    wait("y", cQG)
    wait("y", cQS)
    wait("y", cQC)
    wait("y", cT)
    wait("y", cV)
    wait("y", cS)

    backing = ctx.enter_context(nc.sbuf_tensor("_arena", [128, ((ar.peak + 31) // 32) * 32], U8))
    ar.finalize(backing)
    with nc.Block() as block:
        @block.tensor
        def _(e):
            for f in ops['t']: f(e)
        @block.vector
        def _(e):
            for f in ops['v']: f(e)
        @block.scalar
        def _(e):
            for f in ops['s']: f(e)
        @block.gpsimd
        def _(e):
            for f in ops['g']: f(e)
        @block.sync
        def _(e):
            for f in ops['y']: f(e)
    ctx.close()
    nc._ar = ar
    return nc, dbg_names


def make_inmap(inputs, b):
    ident, tri, esel, e2 = host_consts()
    m = {
        "x": inputs["L0_attn_x"][b],
        "qkv_w": inputs["L0_attn_qkv_w"],
        "qkv_b": inputs["L0_attn_qkv_b"].reshape(1, -1),
        "w1w": inputs["L0_attn_w1w"],
        "w2w": inputs["L0_attn_w2w"],
        "r1w": inputs["L0_attn_r1w"],
        "r2w": inputs["L0_attn_r2w"],
        "memv_w": inputs["L0_attn_memv_w"],
        "memv_b": inputs["L0_attn_memv_b"].reshape(1, -1),
        "memg_w": inputs["L0_attn_memg_w"],
        "memg_b": inputs["L0_attn_memg_b"].reshape(1, -1),
        "mem_scale": inputs["L0_attn_mem_scale"].reshape(16, 1),
        "out_w": inputs["L0_attn_out_w"],
        "out_b": inputs["L0_attn_out_b"].reshape(1, -1),
        "w_gate": inputs["L0_ffn_w_gate"],
        "w_up": inputs["L0_ffn_w_up"],
        "w_down": inputs["L0_ffn_w_down"],
        "c_ident": ident, "c_tri": tri, "c_esel": esel, "c_e2": e2,
    }
    return {k: np.ascontiguousarray(np.asarray(v, dtype=np.float32)) for k, v in m.items()}


def kernel(**inputs):
    nc, _ = build_program()
    in_maps = [make_inmap(inputs, b) for b in range(8)]
    res = run_bass_kernel_spmd(nc, in_maps, core_ids=list(range(8)))
    return np.stack([np.asarray(res.results[b]["out"], np.float32) for b in range(8)], axis=0)
